# version 10
# baseline (speedup 1.0000x reference)
import math
import numpy as np
import concourse.bass as bass
import concourse.mybir as mybir
from concourse.bass_utils import run_bass_kernel_spmd

F32 = mybir.dt.float32
BF16 = mybir.dt.bfloat16
F16 = mybir.dt.float16
U8 = mybir.dt.uint8
AF = mybir.ActivationFunctionType
ALU = mybir.AluOpType

D = 2048
CD = 1024
NH = 16
FF = 8192
KC = D // 128
CDC = CD // 128
ISQ = 1.0 / math.sqrt(128.0)
LN_EPS = 1e-5


def _hb(ap2d, nh):
    return bass.AP(ap2d.tensor, ap2d.offset,
                   [list(ap2d.ap[0]), [0, nh], list(ap2d.ap[1])])


class Sched:

    ENGS = ("pe", "dv", "ac", "sy", "gp")

    def __init__(self, nc):
        self.nc = nc
        self.q = {k: [] for k in self.ENGS}
        self.sems = {}
        self.cnt = {}

    def sem(self, name):
        self.sems[name] = self.nc.alloc_semaphore(name=name)
        self.cnt[name] = 0

    def wait(self, eng, sname, val=None):
        v = self.cnt[sname] if val is None else val
        if v <= 0:
            return
        h = self.sems[sname]
        self.q[eng].append(lambda e, h=h, v=v: e.wait_ge(h, v))

    def drain(self, eng):
        self.q[eng].append(lambda e: e.drain())

    def op(self, eng, fn, inc=None, n=1):
        if inc is None:
            self.q[eng].append(fn)
            return None
        self.cnt[inc] += n
        h = self.sems[inc]
        self.q[eng].append(lambda e, fn=fn, h=h, n=n: fn(e).then_inc(h, n))
        return self.cnt[inc]


def build_nc(n_cores=8, sloc=1024, upto=None):
    ntc = sloc // 128
    th = min(512, sloc)
    nth = sloc // th
    nc = bass.Bass("TRN2", target_bir_lowering=False)

    x_in = nc.dram_tensor("x", [sloc, D], F32, kind="ExternalInput")
    w_dkv = nc.dram_tensor("w_dkv", [D, CD], F32, kind="ExternalInput")
    w_dq = nc.dram_tensor("w_dq", [D, CD], F32, kind="ExternalInput")
    w_uk = nc.dram_tensor("w_uk", [CD, CD], F32, kind="ExternalInput")
    w_uv = nc.dram_tensor("w_uv", [CD, D], F32, kind="ExternalInput")
    w_uq = nc.dram_tensor("w_uq", [CD, CD], F32, kind="ExternalInput")
    w_kr = nc.dram_tensor("w_kr", [D, CD], F32, kind="ExternalInput")
    w_qr = nc.dram_tensor("w_qr", [CD, CD], F32, kind="ExternalInput")
    w_out = nc.dram_tensor("w_out", [D, D], F32, kind="ExternalInput")
    w0t = nc.dram_tensor("w0t", [D, FF], F32, kind="ExternalInput")
    w1t = nc.dram_tensor("w1t", [FF, D], F32, kind="ExternalInput")
    b0 = nc.dram_tensor("b0", [FF], F32, kind="ExternalInput")
    b1 = nc.dram_tensor("b1", [D], F32, kind="ExternalInput")
    cos_in = nc.dram_tensor("cos", [128, ntc, 16], F32, kind="ExternalInput")
    sin_in = nc.dram_tensor("sin", [128, ntc, 16], F32, kind="ExternalInput")
    id_in = nc.dram_tensor("idm", [128, 128], F32, kind="ExternalInput")
    y_out = nc.dram_tensor("y", [sloc, D], F32, kind="ExternalOutput")
    cc_in = nc.dram_tensor("cc_in", [128, NH, 128], F32)
    cc_out = nc.dram_tensor("cc_out", [128, NH, 128], F32)
    dsp = nc.dram_tensor("dsp", [FF, sloc], BF16)

    KB = 1024
    o_cos, o_sin = 0, 512
    o_id = 1024
    o_stats = 1280
    o_b0 = o_stats + 512
    o_ra = o_b0 + 256
    o_rb = o_ra + 512
    o_htmp = o_rb + 512
    o_mst = o_htmp + 8 * KB
    o_msbf = o_mst + 8 * KB
    o_ones = o_msbf + 4 * KB
    o_b1bf = o_ones + 256
    R0_END = o_b1bf + 4 * KB + 768
    o_hfm = R0_END
    o_ckv = o_hfm + 2 * KC * sloc
    o_cq = o_ckv + 2 * CDC * sloc
    o_krope = o_cq
    o_qc = o_cq + 2 * CDC * sloc
    o_krstg = o_qc
    o_qtm = o_qc + 2 * CDC * sloc
    o_xld = o_qtm + 2 * ntc * 16 * 128
    o_xsqA = o_xld + 16 * KB
    o_vtm = o_xld
    o_kboth = o_xld
    o_wsl = o_xld + max(24 * KB, 2 * ntc * 2048, 2 * 16 * sloc)
    TOTAL = o_wsl + 32 * KB
    o_w1r_x = None
    o_wkra_x = None
    if o_wsl - o_xld >= 32 * KB:
        o_wdq = o_xld
    else:
        o_wdq = None
    assert TOTAL <= 207 * KB, TOTAL
    if sloc >= 1024:
        o_apre = o_hfm
        o_x1 = o_ckv
        assert o_x1 + 4 * ntc * 2048 <= o_xld + 24 * KB
        o_gtm = o_x1 + 4 * ntc * 2048
        o_gfm = o_kboth
        o_wout = o_wsl
        o_xsqB = o_wsl + 16 * KB
        o_b1bc = o_wsl
        o_ef16 = o_wsl + 8 * KB
        o_w0r = o_apre
        o_w1r = o_apre
        o_dst = o_gtm
        o_drg = o_gtm
    else:
        cur = [TOTAL]

        def _al(szb):
            o = cur[0]
            cur[0] += (szb + 31) // 32 * 32
            return o

        o_apre = _al(2 * 16 * sloc)
        o_x1 = _al(4 * ntc * 2048)
        o_gtm = _al(8 * KB)
        o_gfm = _al(2 * 16 * sloc)
        o_shared = _al(32 * KB)
        o_wdq = o_shared
        o_wout = o_shared
        o_w0r = o_shared
        o_w1r = o_shared
        o_xsqB = _al(8 * KB)
        o_dst = _al(2 * 8 * sloc)
        o_drg = _al(2 * 8 * sloc)
        o_wuk_x = _al(16 * KB)
        o_woutb_x = o_wuk_x
        o_w1r_x = o_wuk_x
        o_wkra_x = _al(16 * KB)
        TOTAL = cur[0]
        assert TOTAL <= 207 * KB, TOTAL

    with (
        nc.sbuf_tensor("arena", [128, TOTAL], U8) as arena,
        nc.psum_tensor("ps", [128, 8, 512], F32) as ps,
        nc.Block() as block,
    ):
        base = nc.lookup_mloc(arena).addr
        ps_addr = nc.lookup_mloc(ps).addr
        pstr = nc.virtual_psum_tensor("pstr", [128, 2, 1024], BF16)
        nc._tensor(pstr.name, [128, 2, 1024], BF16, type="PSUM")
        _m = nc.lookup_mloc(pstr)
        _m.allocated = True
        _m.addr = ps_addr + 6 * 2048

        def sb(name, off, shape, dtype):
            h = nc.virtual_sbuf_tensor(name, shape, dtype)
            nc._tensor(h.name, list(shape), dtype, type="SB")
            mloc = nc.lookup_mloc(h)
            mloc.allocated = True
            mloc.addr = base + off
            return h

        cos_sb = sb("cos_sb", o_cos, [128, ntc, 16], F32)
        sin_sb = sb("sin_sb", o_sin, [128, ntc, 16], F32)
        ident = sb("ident", o_id, [128, 128], BF16)
        stats = sb("stats", o_stats, [128, 128], F32)
        b0sb = sb("b0sb", o_b0, [128, 64], F32)
        ropeA = sb("ropeA", o_ra, [128, 8, 16], F32)
        ropeB = sb("ropeB", o_rb, [128, 8, 16], F32)
        htmp = sb("htmp", o_htmp, [128, 2, 2048], BF16)
        mstage = sb("mstage", o_mst, [128, NH, 128], F32)
        msb = sb("msb", o_mst, [128, NH, 128], F32)
        msbf = sb("msbf", o_msbf, [128, NH, 128], BF16)
        hfm = sb("hfm", o_hfm, [128, KC, sloc], BF16)
        ckv = sb("ckv", o_ckv, [128, CDC, sloc], BF16)
        cq = sb("cq", o_cq, [128, CDC, sloc], BF16)
        qc = sb("qc", o_qc, [128, CDC, sloc], BF16)
        qtm = sb("qtm", o_qtm, [128, ntc, NH, 128], BF16)
        krope = sb("krope", o_krope, [128, ntc, NH, 64], BF16)
        krstg = sb("krstg", o_krstg, [64, NH, sloc], BF16)
        xld = sb("xld", o_xld, [128, 4, 2048], F32)
        vtm = sb("vtm", o_vtm, [128, ntc, 2048], BF16)
        kboth = sb("kboth", o_kboth, [128, NH, sloc], BF16)
        wsl = sb("wsl", o_wsl, [128, 16, 1024], BF16)
        wdq_sb = sb("wdq_sb", o_wdq, [128, 16, 1024], BF16)
        wkr_a = sb("wkr_a", (o_qc if sloc >= 1024 else o_wkra_x), [128, 16, 512], BF16)
        wuv_a = sb("wuv_a", o_wsl, [128, CDC, 1024], BF16)
        wuv_b = sb("wuv_b", o_wsl + 16 * KB, [128, CDC, 1024], BF16)
        ones_sb = sb("ones_sb", o_ones, [1, 128], BF16)
        b1bf = sb("b1bf", o_b1bf, [1, D], BF16)
        wuq_sb = sb("wuq_sb", o_wsl, [128, CDC, 1024], BF16)
        wqr_sb = sb("wqr_sb", o_wsl + 16 * KB, [128, CDC, 1024], BF16)
        wuv_sb = sb("wuv_sb", o_wsl, [128, CDC, 2048], BF16)
        wuk_sb = sb("wuk_sb", (o_gtm if sloc >= 1024 else o_wuk_x), [128, CDC, 1024], BF16)
        apre = sb("apre", o_apre, [128, NH, sloc], BF16)
        x1 = sb("x1", o_x1, [128, ntc, 2048], F32)
        gtm = sb("gtm", o_gtm, [128, 2, 2048], BF16)
        gfm = sb("gfm", o_gfm, [128, KC, sloc], BF16)
        wouts = sb("wouts", o_wout, [128, 16, 512], BF16)
        wouts_b = sb("wouts_b", (o_gtm if sloc >= 1024 else o_woutb_x), [128, 16, 512], BF16)
        xsqB = sb("xsqB", o_xsqB, [128, 2048], F32)
        w0r = sb("w0r", o_w0r, [128, 2, 16, 512], BF16)
        w1r = sb("w1r", (o_wsl + 16 * KB if sloc >= 1024 else o_w1r_x), [128, 2, 8, 512], BF16)
        dst = sb("dst", o_dst, [128, 8, sloc], BF16)
        drg = sb("drg", (o_wsl if sloc >= 1024 else o_drg), [128, 2, 4, sloc], BF16)


        s = Sched(nc)
        for nm in ("pe", "dv", "ac", "gw0", "gw1", "gw2", "gw3", "idl", "tbl",
                   "xa", "xb", "cc", "cci", "msb", "krs", "x1i", "dsa", "dsb",
                   "w0a", "w0b", "w1a", "w1b", "dra", "drb", "yst", "b1l"):
            s.sem(nm)
        _gwi = [0]

        def wload(fn):
            pn = "gw%d" % (_gwi[0] % 4)
            _gwi[0] += 1
            return (pn, s.op("gp", fn, inc=pn, n=16))

        _finish = [None]

        def _mk_finish():
            def fin():
                @block.tensor
                def _(t):
                    for f in s.q["pe"]:
                        f(t)

                @block.vector
                def _(v):
                    for f in s.q["dv"]:
                        f(v)

                @block.scalar
                def _(a):
                    for f in s.q["ac"]:
                        f(a)

                @block.sync
                def _(sy):
                    for f in s.q["sy"]:
                        f(sy)

                @block.gpsimd
                def _(g):
                    for f in s.q["gp"]:
                        f(g)
            return fin
        _finish[0] = _mk_finish()

        class _Cut(Exception):
            pass

        def cut(tag):
            if upto == tag:
                raise _Cut()

        bank_free = [("dv", 0)] * 8
        slot_free = [("dv", 0)] * 8
        EPS_AP = stats[:, 120:121]
        try:

            def pe_wait_bank(b):
                sm, v = bank_free[b]
                s.wait("pe", sm, v)

            def mm(psum_ap, lhsT, rhs, start, stop, inc=None):
                return s.op("pe", lambda e, a=psum_ap, l=lhsT, r=rhs, st=start,
                            sp=stop: e.matmul(a, l, r, start=st, stop=sp), inc=inc)

            tbank_free = {6: ("dv", 0), 7: ("dv", 0)}

            def transpose_groups(n_tiles, src_fn, dst_fn, rows=128, upper=False):
                p0 = 64 if upper else 0
                tp = (0, 64) if upper else None
                ngr = (n_tiles + 7) // 8
                for g in range(ngr):
                    bk = 6 + (g % 2)
                    area = pstr[:, g % 2, :]
                    sm, v = tbank_free[bk]
                    s.wait("pe", sm, v)
                    lo = g * 8
                    hi = min(lo + 8, n_tiles)
                    for i in range(lo, hi):
                        sl = i - lo
                        s.op("pe", lambda e, i=i, sl=sl, area=area: e.transpose(
                            area[p0:p0 + rows, sl * 128:(sl + 1) * 128], src_fn(i),
                            ident[:, :], tile_position=tp),
                            inc="pe" if i == hi - 1 else None)
                    pm = s.cnt["pe"]
                    s.wait("dv", "pe", pm)
                    for i in range(lo, hi):
                        sl = i - lo
                        s.op("dv", lambda e, i=i, sl=sl, area=area: e.tensor_copy(
                            dst_fn(i), area[p0:p0 + rows, sl * 128:(sl + 1) * 128]),
                            inc="dv" if i == hi - 1 else None)
                    tbank_free[bk] = ("dv", s.cnt["dv"])

            s.op("dv", lambda e: e.memset(stats[:, 120:121], LN_EPS), inc="dv")
            s.op("gp", lambda e: e.dma_start(out=ident[:, :], in_=id_in[:, :]), inc="idl", n=16)
            s.op("sy", lambda e: e.dma_start(out=cos_sb[:, :, :], in_=cos_in[:, :, :]), inc="tbl", n=16)
            s.op("sy", lambda e: e.dma_start(out=sin_sb[:, :, :], in_=sin_in[:, :, :]), inc="tbl", n=16)
            tbl_all = s.cnt["tbl"]
            s.op("dv", lambda e: e.memset(ones_sb[0:1, :], 1.0))
            cut("A0")

            x3 = x_in.ap().rearrange("(tc p) d -> p tc d", p=128)
            htmp_free = [("pe", 0), ("pe", 0)]
            SP, SQ, SM, SI, SV, SS2 = 0, 8, 16, 24, 32, 40
            p1_rd = {}
            for tc in range(ntc):
                par = tc % 4 if ntc >= 4 else tc % 2
                qeng, qsem = ("sy", "xa") if tc % 2 == 0 else ("ac", "xb")
                dep = 4 if ntc >= 4 else 2
                if tc >= dep:
                    a_c, d_c = p1_rd[tc - dep]
                    s.wait(qeng, "ac", a_c)
                    s.wait(qeng, "dv", d_c)
                xw = s.op(qeng, lambda e, t=tc, p=par: e.dma_start(
                    out=xld[:, p, :], in_=x3[:, t, :]), inc=qsem, n=16)
                s.wait("ac", qsem, xw)
                s.drain("ac")
                s.op("ac", lambda e, p=par, t=tc: e.activation(
                    htmp[:, p % 2, :], xld[:, p, :], AF.Square,
                    accum_out=stats[:, SQ + t:SQ + t + 1]), inc="ac")
                sq_done = s.cnt["ac"]
                s.wait("dv", qsem, xw)
                s.op("dv", lambda e, p=par, t=tc: e.reduce_sum(
                    stats[:, SP + t:SP + t + 1], xld[:, p, :],
                    axis=mybir.AxisListType.X), inc="dv")
                p1_rd[tc] = (sq_done, s.cnt["dv"])
            s.wait("gp", "xa", s.cnt["xa"])
            ld_dkv = wload(lambda e: e.dma_start(
                out=wsl[:, :, :], in_=w_dkv.ap().rearrange("(k p) m -> p k m", p=128)))
            s.wait("dv", "ac", s.cnt["ac"])
            s.drain("dv")
            for tc in range(ntc):
                s.op("dv", lambda e, t=tc: e.tensor_single_scalar(
                    stats[:, SM + t:SM + t + 1], stats[:, SP + t:SP + t + 1],
                    1.0 / D, ALU.mult))
                s.op("dv", lambda e, t=tc: e.tensor_single_scalar(
                    stats[:, SS2 + t:SS2 + t + 1], stats[:, SQ + t:SQ + t + 1],
                    1.0 / D, ALU.mult))
            s.drain("dv")
            for tc in range(ntc):
                s.op("dv", lambda e, t=tc: e.tensor_tensor(
                    stats[:, SV + t:SV + t + 1], stats[:, SM + t:SM + t + 1],
                    stats[:, SM + t:SM + t + 1], ALU.mult))
            s.drain("dv")
            s.op("dv", lambda e: e.tensor_tensor(
                stats[:, SV:SV + ntc], stats[:, SS2:SS2 + ntc],
                stats[:, SV:SV + ntc], ALU.subtract), inc="dv")
            s.wait("ac", "dv", s.cnt["dv"])
            s.op("ac", lambda e: e.activation(
                stats[:, SS2:SS2 + ntc], stats[:, SV:SV + ntc], AF.Sqrt,
                bias=EPS_AP), inc="ac")
            s.wait("dv", "ac", s.cnt["ac"])
            s.op("dv", lambda e: e.reciprocal(
                stats[:, SI:SI + ntc], stats[:, SS2:SS2 + ntc]), inc="dv")
            s.drain("dv")
            inv_ready_dv = s.cnt["dv"]
            p1_ac_all = s.cnt["ac"]
            p2_norm = {}
            for tc in range(ntc):
                par = tc % 4 if ntc >= 4 else tc % 2
                qeng, qsem = ("sy", "xa") if tc % 2 == 0 else ("ac", "xb")
                dep = 4 if ntc >= 4 else 2
                if tc >= dep:
                    s.wait(qeng, "dv", p2_norm[tc - dep])
                else:
                    s.wait(qeng, "dv", inv_ready_dv)
                    s.wait(qeng, "ac", p1_ac_all)
                xw = s.op(qeng, lambda e, t=tc, p=par: e.dma_start(
                    out=xld[:, p, :], in_=x3[:, t, :]), inc=qsem, n=16)
                hp = par % 2
                sm, v = htmp_free[hp]
                s.wait("dv", sm, v)
                s.wait("dv", qsem, xw)
                s.op("dv", lambda e, p=par, hp=hp, t=tc: e.tensor_scalar(
                    htmp[:, hp, :], xld[:, p, :], stats[:, SM + t:SM + t + 1],
                    stats[:, SI + t:SI + t + 1], ALU.subtract, ALU.mult), inc="dv")
                h_done = s.cnt["dv"]
                p2_norm[tc] = h_done
                s.wait("pe", "dv", h_done)
                if tc == 0:
                    s.wait("pe", "idl", 16)
                transpose_groups(
                    KC,
                    lambda i, hp=hp: htmp[:, hp, i * 128:(i + 1) * 128],
                    lambda i, t=tc: hfm[:, i, t * 128:(t + 1) * 128])
                htmp_free[hp] = ("pe", s.cnt["pe"])
            h_ready = s.cnt["dv"]
            s.op("sy", lambda e: e.dma_start(
                out=b0sb[:, :], in_=b0.ap().rearrange("(c p) -> p c", p=128),
                allow_slow_non_contiguous=True), inc="tbl", n=16)
            b0_cnt = s.cnt["tbl"]
            cut("A1")

            def proj_fm(dstT, w_sb, n_out_chunks, n_k, rhs_sb):
                for m in range(n_out_chunks):
                    for t in range(nth):
                        b = (m * nth + t) % 4
                        pe_wait_bank(b)
                        for k in range(n_k):
                            mm(ps[:, b, 0:th],
                               w_sb[:, k, m * 128:(m + 1) * 128],
                               rhs_sb[:, k, t * th:(t + 1) * th],
                               start=(k == 0), stop=(k == n_k - 1),
                               inc="pe" if k == n_k - 1 else None)
                        pm = s.cnt["pe"]
                        s.wait("dv", "pe", pm)
                        s.op("dv", lambda e, m=m, t=t, b=b: e.tensor_copy(
                            dstT[:, m, t * th:(t + 1) * th], ps[:, b, 0:th]), inc="dv")
                        bank_free[b] = ("dv", s.cnt["dv"])

            s.wait("gp", "dv", h_ready)
            s.wait("gp", "ac", s.cnt["ac"])
            ld_dq = wload(lambda e: e.dma_start(
                out=wdq_sb[:, :, :], in_=w_dq.ap().rearrange("(k p) m -> p k m", p=128)))
            s.wait("pe", *ld_dkv)
            s.wait("pe", "dv", h_ready)
            proj_fm(ckv, wsl, CDC, KC, hfm)
            ckv_done_pe = s.cnt["pe"]
            cut("A2")

            s.wait("gp", "pe", ckv_done_pe)
            ld_uq = wload(lambda e: e.dma_start(
                out=wuq_sb[:, :, :], in_=w_uq.ap().rearrange("(k p) m -> p k m", p=128)))
            ld_qr = wload(lambda e: e.dma_start(
                out=wqr_sb[:, :, :], in_=w_qr.ap().rearrange("(k p) m -> p k m", p=128)))
            s.wait("pe", *ld_dq)
            proj_fm(cq, wdq_sb, CDC, KC, hfm)
            cq_done_pe = s.cnt["pe"]

            s.wait("pe", *ld_uq)
            s.wait("pe", "dv")
            proj_fm(qc, wuq_sb, CDC, CDC, cq)
            cut("A4")

            for tc in range(ntc):
                for g in range(2):
                    b = (tc * 2 + g) % 4
                    pe_wait_bank(b)
                    for k in range(CDC):
                        mm(ps[:, b, 0:512], cq[:, k, tc * 128:(tc + 1) * 128],
                           wuq_sb[:, k, g * 512:(g + 1) * 512],
                           start=(k == 0), stop=(k == CDC - 1),
                           inc="pe" if k == CDC - 1 else None)
                    pm = s.cnt["pe"]
                    s.wait("dv", "pe", pm)
                    s.op("dv", lambda e, tc=tc, g=g, b=b: e.tensor_copy(
                        qtm[:, tc, 8 * g:8 * g + 8, 0:64],
                        ps[:, b, 0:512].rearrange("p (h c) -> p h c", h=8)), inc="dv")
                    bank_free[b] = ("dv", s.cnt["dv"])
            a5_done_pe = s.cnt["pe"]
            s.wait("gp", "pe", a5_done_pe)
            ld_uva = wload(lambda e: e.dma_start(
                out=wuv_a[:, :, :],
                in_=w_uv.ap().rearrange("(k p) m -> p k m", p=128)[:, :, 0:1024]))

            def rope_block(psum_ap, out_ap, tc):
                P = psum_ap.rearrange("p (h c) -> p h c", h=8)
                cl = _hb(cos_sb[:, tc, :], 8)
                sl_ = _hb(sin_sb[:, tc, :], 8)
                s.drain("dv")
                s.op("dv", lambda e, P=P, cl=cl: e.tensor_tensor(
                    ropeA[:, :, :], P[:, :, 0:16], cl, ALU.mult))
                s.op("dv", lambda e, P=P, sl_=sl_: e.tensor_tensor(
                    ropeB[:, :, :], P[:, :, 16:32], sl_, ALU.mult))
                s.drain("dv")
                s.op("dv", lambda e, out_ap=out_ap: e.tensor_tensor(
                    out_ap[:, :, 0:16], ropeA[:, :, :], ropeB[:, :, :], ALU.subtract))
                s.drain("dv")
                s.op("dv", lambda e, P=P, cl=cl: e.tensor_tensor(
                    ropeA[:, :, :], P[:, :, 16:32], cl, ALU.mult))
                s.op("dv", lambda e, P=P, sl_=sl_: e.tensor_tensor(
                    ropeB[:, :, :], P[:, :, 0:16], sl_, ALU.mult))
                s.drain("dv")
                s.op("dv", lambda e, out_ap=out_ap: e.tensor_tensor(
                    out_ap[:, :, 16:32], ropeA[:, :, :], ropeB[:, :, :], ALU.add))
                s.op("dv", lambda e, P=P, out_ap=out_ap: e.tensor_copy(
                    out_ap[:, :, 32:64], P[:, :, 32:64]), inc="dv")

            s.wait("dv", "tbl", b0_cnt)
            s.wait("pe", *ld_qr)
            s.wait("pe", "dv")
            for tc in range(ntc):
                for g in range(2):
                    b = (tc * 2 + g) % 4
                    pe_wait_bank(b)
                    for k in range(CDC):
                        mm(ps[:, b, 0:512], qc[:, k, tc * 128:(tc + 1) * 128],
                           wqr_sb[:, k, g * 512:(g + 1) * 512],
                           start=(k == 0), stop=(k == CDC - 1),
                           inc="pe" if k == CDC - 1 else None)
                    pm = s.cnt["pe"]
                    s.wait("dv", "pe", pm)
                    rope_block(ps[:, b, 0:512], qtm[:, tc, 8 * g:8 * g + 8, 64:128], tc)
                    bank_free[b] = ("dv", s.cnt["dv"])
            qr_done_pe = s.cnt["pe"]
            s.wait("gp", "pe", qr_done_pe)
            ld_kra = wload(lambda e: e.dma_start(
                out=wkr_a[:, :, :],
                in_=w_kr.ap().rearrange("(k p) m -> p k m", p=128)[:, :, 0:512]))
            cut("A6")

            s.wait("gp", "pe", qr_done_pe)
            ld_uvb = wload(lambda e: e.dma_start(
                out=wuv_b[:, :, :],
                in_=w_uv.ap().rearrange("(k p) m -> p k m", p=128)[:, :, 1024:2048]))
            v3 = vtm.ap().rearrange("p tc (vc c) -> p tc vc c", c=512)
            for vc in range(4):
                s.wait("pe", *(ld_uva if vc < 2 else ld_uvb))
                wv = wuv_a if vc < 2 else wuv_b
                vcl = vc % 2
                for tc in range(ntc):
                    b = (vc * ntc + tc) % 4
                    pe_wait_bank(b)
                    for k in range(CDC):
                        mm(ps[:, b, 0:512], ckv[:, k, tc * 128:(tc + 1) * 128],
                           wv[:, k, vcl * 512:(vcl + 1) * 512],
                           start=(k == 0), stop=(k == CDC - 1),
                           inc="pe" if k == CDC - 1 else None)
                    pm = s.cnt["pe"]
                    s.wait("dv", "pe", pm)
                    s.op("dv", lambda e, tc=tc, vc=vc, b=b: e.tensor_copy(
                        v3[:, tc, vc, :], ps[:, b, 0:512]), inc="dv")
                    bank_free[b] = ("dv", s.cnt["dv"])
            v_done_pe = s.cnt["pe"]
            cut("A7")

            s.wait("pe", "dv", s.cnt["dv"])
            for hp in range(NH // 2):
                pe_wait_bank(4)
                pe_wait_bank(5)
                for j in range(2):
                    h = 2 * hp + j
                    for tc in range(ntc):
                        mm(ps[:, 4 + j, 0:128], qtm[:, tc, h, :],
                           vtm[:, tc, h * 128:(h + 1) * 128],
                           start=(tc == 0), stop=(tc == ntc - 1),
                           inc="pe" if (j == 1 and tc == ntc - 1) else None)
                pm = s.cnt["pe"]
                s.wait("dv", "pe", pm)
                for j in range(2):
                    h = 2 * hp + j
                    s.op("dv", lambda e, h=h, j=j: e.tensor_single_scalar(
                        mstage[:, h, :], ps[:, 4 + j, 0:128], ISQ, ALU.mult),
                        inc="dv" if j == 1 else None)
                bank_free[4] = ("dv", s.cnt["dv"])
                bank_free[5] = ("dv", s.cnt["dv"])
            m_evac_done = s.cnt["dv"]
            cut("A8")

            m_done_pe = s.cnt["pe"]
            s.wait("sy", "dv", m_evac_done)
            s.op("sy", lambda e: e.dma_start(out=cc_in[:, :, :], in_=mstage[:, :, :]),
                 inc="cci", n=16)
            s.wait("gp", "pe", v_done_pe)
            ld_krb = wload(lambda e: e.dma_start(
                out=wsl[:, :, 0:512],
                in_=w_kr.ap().rearrange("(k p) m -> p k m", p=128)[:, :, 512:1024]))
            s.wait("gp", "cci", 16)
            s.op("gp", lambda e: e.collective_compute(
                "AllReduce", ALU.add, replica_groups=[list(range(n_cores))],
                ins=[cc_in.ap().opt()], outs=[cc_out.ap().opt()]), inc="cc")
            s.wait("sy", "cc", 1)
            s.op("sy", lambda e: e.dma_start(out=msb[:, :, :], in_=cc_out[:, :, :]),
                 inc="msb", n=16)
            s.wait("gp", "pe", m_done_pe)
            ld_uk = wload(lambda e: e.dma_start(
                out=wuk_sb[:, :, :], in_=w_uk.ap().rearrange("(k p) m -> p k m", p=128)))
            ld_b1 = ("b1l", s.op("gp", lambda e: e.dma_start(
                out=b1bf[0:1, :], in_=b1.ap().unsqueeze(0)), inc="b1l", n=16))

            for g in range(2):
                s.wait("pe", *(ld_kra if g == 0 else ld_krb))
                wk = wkr_a if g == 0 else wsl
                for tc in range(ntc):
                    b = (g * ntc + tc) % 4
                    pe_wait_bank(b)
                    for k in range(KC):
                        mm(ps[:, b, 0:512], hfm[:, k, tc * 128:(tc + 1) * 128],
                           wk[:, k, 0:512],
                           start=(k == 0), stop=(k == KC - 1),
                           inc="pe" if k == KC - 1 else None)
                    pm = s.cnt["pe"]
                    s.wait("dv", "pe", pm)
                    rope_block(ps[:, b, 0:512], krope[:, tc, 8 * g:8 * g + 8, 0:64], tc)
                    bank_free[b] = ("dv", s.cnt["dv"])
            kr_done_pe = s.cnt["pe"]
            krope_done_dv = s.cnt["dv"]
            cut("A11")

            s.wait("pe", "dv", krope_done_dv)
            transpose_groups(
                ntc * NH,
                lambda i: krope[:, i // NH, i % NH, :],
                lambda i: kboth[64:128, i % NH, (i // NH) * 128:(i // NH) * 128 + 128],
                rows=64, upper=True)
            krstg_done_dv = s.cnt["dv"]
            krT_done_pe = s.cnt["pe"]
            cut("A12")

            s.wait("pe", *ld_uk)
            for h in range(NH):
                for t in range(nth):
                    b = (h * nth + t) % 4
                    pe_wait_bank(b)
                    for k in range(CDC):
                        mm(ps[0:64, b, 0:th], wuk_sb[:, k, h * 64:(h + 1) * 64],
                           ckv[:, k, t * th:(t + 1) * th],
                           start=(k == 0), stop=(k == CDC - 1),
                           inc="pe" if k == CDC - 1 else None)
                    pm = s.cnt["pe"]
                    s.wait("ac", "pe", pm)
                    s.op("ac", lambda e, h=h, t=t, b=b: e.copy(
                        kboth[0:64, h, t * th:(t + 1) * th], ps[0:64, b, 0:th]), inc="ac")
                    bank_free[b] = ("ac", s.cnt["ac"])
            kc_done_pe = s.cnt["pe"]
            kc_done_ac = s.cnt["ac"]
            cut("A10")

            s.wait("sy", "dv", krstg_done_dv)
            s.wait("sy", "pe", kc_done_pe)
            s.op("sy", lambda e: e.dma_start(out=x1[:, :, :], in_=x3[:, :, :]),
                 inc="x1i", n=16)

            s.wait("dv", "msb", 16)
            s.op("dv", lambda e: e.tensor_copy(msbf[:, :, :], msb[:, :, :]), inc="dv")
            msbf_done = s.cnt["dv"]
            cut("AR")

            s.wait("pe", "dv", msbf_done)
            units = [(h, t) for h in range(NH) for t in range(nth)]
            for g0 in range(0, len(units), 8):
                grp = units[g0:g0 + 8]
                for b in range(min(8, len(grp))):
                    pe_wait_bank(b)
                for i, (h, t) in enumerate(grp):
                    mm(ps[:, i, 0:th], msbf[:, h, :], kboth[:, h, t * th:(t + 1) * th],
                       start=True, stop=True,
                       inc="pe" if i == len(grp) - 1 else None)
                pm = s.cnt["pe"]
                s.wait("dv", "pe", pm)
                s.wait("ac", "pe", pm)
                for i, (h, t) in enumerate(grp):
                    eng = "dv" if i % 2 == 0 else "ac"
                    fn = (lambda e, h=h, t=t, i=i: e.tensor_copy(
                        apre[:, h, t * th:(t + 1) * th], ps[:, i, 0:th])) \
                        if eng == "dv" else \
                        (lambda e, h=h, t=t, i=i: e.copy(
                            apre[:, h, t * th:(t + 1) * th], ps[:, i, 0:th]))
                    s.op(eng, fn, inc=eng if i >= len(grp) - 2 else None)
                for b in range(min(8, len(grp))):
                    bank_free[b] = ("dv", s.cnt["dv"]) if b % 2 == 0 else ("ac", s.cnt["ac"])

            s.wait("pe", "dv")
            s.wait("pe", "ac")
            wo4 = w_out.ap().rearrange("(m p) o -> p m o", p=128)
            wout_marker = {-2: kr_done_pe, -1: kc_done_pe}
            wout_loaded = {}

            def load_wout(og):
                s.wait("gp", "pe", wout_marker[og - 2])
                buf = wouts if og % 2 == 0 else wouts_b
                wout_loaded[og] = wload(lambda e, og=og, buf=buf: e.dma_start(
                    out=buf[:, :, :], in_=wo4[:, :, og * 512:(og + 1) * 512]))

            load_wout(0)
            load_wout(1)
            s.wait("dv", "x1i", 16)
            for og in range(4):
                s.wait("pe", *wout_loaded[og])
                wbuf = wouts if og % 2 == 0 else wouts_b
                for tc in range(ntc):
                    b = (og * ntc + tc) % 4
                    pe_wait_bank(b)
                    for m in range(16):
                        mm(ps[:, b, 0:512], apre[:, m, tc * 128:(tc + 1) * 128],
                           wbuf[:, m, :],
                           start=(m == 0), stop=(m == 15),
                           inc="pe" if m == 15 else None)
                    pm = s.cnt["pe"]
                    s.wait("dv", "pe", pm)
                    s.op("dv", lambda e, tc=tc, og=og, b=b: e.tensor_tensor(
                        x1[:, tc, og * 512:(og + 1) * 512],
                        x1[:, tc, og * 512:(og + 1) * 512],
                        ps[:, b, 0:512], ALU.add), inc="dv")
                    add_done = s.cnt["dv"]
                    bank_free[b] = ("dv", add_done)
                    s.drain("dv")
                    s.op("dv", lambda e, tc=tc, og=og: e.reduce_sum(
                        stats[:, 48 + tc * 4 + og:48 + tc * 4 + og + 1],
                        x1[:, tc, og * 512:(og + 1) * 512],
                        axis=mybir.AxisListType.X), inc="dv")
                    s.wait("ac", "dv", add_done)
                    s.drain("ac")
                    s.op("ac", lambda e, tc=tc, og=og: e.activation(
                        xsqB[:, 0:512], x1[:, tc, og * 512:(og + 1) * 512],
                        AF.Square,
                        accum_out=stats[:, 80 + tc * 4 + og:80 + tc * 4 + og + 1]),
                        inc="ac")
                wout_marker[og] = s.cnt["pe"]
                if og + 2 < 4:
                    load_wout(og + 2)
            b12_done_pe_v = s.cnt["pe"]
            attn_done_dv = s.cnt["dv"]
            cut("B12")

            ln2_parts_dv = s.cnt["dv"]
            ln2_parts_ac = s.cnt["ac"]
            s.drain("dv")
            for tc in range(ntc):
                s.op("dv", lambda e, t=tc: e.reduce_sum(
                    stats[:, SP + t:SP + t + 1], stats[:, 48 + t * 4:48 + t * 4 + 4],
                    axis=mybir.AxisListType.X))
            s.wait("dv", "ac", ln2_parts_ac)
            for tc in range(ntc):
                s.op("dv", lambda e, t=tc: e.reduce_sum(
                    stats[:, SQ + t:SQ + t + 1], stats[:, 80 + t * 4:80 + t * 4 + 4],
                    axis=mybir.AxisListType.X))
            s.drain("dv")
            for tc in range(ntc):
                s.op("dv", lambda e, t=tc: e.tensor_single_scalar(
                    stats[:, SM + t:SM + t + 1], stats[:, SP + t:SP + t + 1],
                    1.0 / D, ALU.mult))
                s.op("dv", lambda e, t=tc: e.tensor_single_scalar(
                    stats[:, SS2 + t:SS2 + t + 1], stats[:, SQ + t:SQ + t + 1],
                    1.0 / D, ALU.mult))
            s.drain("dv")
            for tc in range(ntc):
                s.op("dv", lambda e, t=tc: e.tensor_tensor(
                    stats[:, SV + t:SV + t + 1], stats[:, SM + t:SM + t + 1],
                    stats[:, SM + t:SM + t + 1], ALU.mult))
            s.drain("dv")
            s.op("dv", lambda e: e.tensor_tensor(
                stats[:, SV:SV + ntc], stats[:, SS2:SS2 + ntc],
                stats[:, SV:SV + ntc], ALU.subtract), inc="dv")
            s.wait("ac", "dv", s.cnt["dv"])
            s.op("ac", lambda e: e.activation(
                stats[:, SS2:SS2 + ntc], stats[:, SV:SV + ntc], AF.Sqrt,
                bias=EPS_AP), inc="ac")
            s.wait("dv", "ac", s.cnt["ac"])
            s.op("dv", lambda e: e.reciprocal(
                stats[:, SI:SI + ntc], stats[:, SS2:SS2 + ntc]), inc="dv")
            s.drain("dv")
            gtm_free = [("pe", 0), ("pe", 0)]
            for tc in range(ntc):
                par = tc % 2
                sm, v = gtm_free[par]
                s.wait("dv", sm, v)
                s.op("dv", lambda e, t=tc, p=par: e.tensor_scalar(
                    gtm[:, p, :], x1[:, t, :], stats[:, SM + t:SM + t + 1],
                    stats[:, SI + t:SI + t + 1], ALU.subtract, ALU.mult), inc="dv")
                g_done = s.cnt["dv"]
                s.wait("pe", "dv", g_done)
                transpose_groups(
                    KC,
                    lambda i, p=par: gtm[:, p, i * 128:(i + 1) * 128],
                    lambda i, t=tc: gfm[:, i, t * 128:(t + 1) * 128])
                gtm_free[par] = ("pe", s.cnt["pe"])
            g_ready_dv = s.cnt["dv"]
            b13_pe = s.cnt["pe"]
            b13_ac = s.cnt["ac"]
            cut("B13")

            w0v = w0t.ap().rearrange("(k p) n -> p k n", p=128)
            NS = FF // 512
            b12_done_pe = b12_done_pe_v
            w0_marker = {-2: b12_done_pe, -1: b12_done_pe}
            dst_free = [("ac", 0), ("ac", 0)]
            dsp4 = dsp.ap().rearrange("(b n4 p) t -> p b n4 t", p=128, n4=4)

            w0_loaded = {}
            store_done = {}

            def load_w0(strip):
                pn = "w0a" if strip % 2 == 0 else "w0b"
                s.wait("gp", "pe", w0_marker[strip - 2])
                w0_loaded[strip] = (pn, s.op(
                    "gp", lambda e, st=strip, p=strip % 2: e.dma_start(
                        out=w0r[:, p, :, :], in_=w0v[:, :, st * 512:(st + 1) * 512]),
                    inc=pn, n=16))

            load_w0(0)
            load_w0(1)
            y3s = y_out.ap().rearrange("(tc p) d -> p tc d", p=128)
            w1v = w1t.ap().rearrange("(g8 n2 p) h -> p g8 n2 h", p=128, n2=8)
            NW1 = 4 * (FF // 1024)
            w1_marker = {-2: b13_pe, -1: b13_pe}
            dr_marker = {-2: b13_pe, -1: b13_pe}
            w1_loaded = {}
            dr_loaded = {}

            def load_w1(gid):
                pn = "w1a" if gid % 2 == 0 else "w1b"
                s.wait("gp", "pe", w1_marker[gid - 2])
                if gid < 2:
                    s.wait("gp", "ac", b13_ac)
                w1_loaded[gid] = (pn, s.op(
                    "gp", lambda e, gid=gid: e.dma_start(
                        out=w1r[:, gid % 2, :, :],
                        in_=w1v[:, gid % 8, :, (gid // 8) * 512:(gid // 8) * 512 + 512]),
                    inc=pn, n=16))

            def ffn2_load_dr(qid):
                pn = "dra" if qid % 2 == 0 else "drb"
                s.wait("sy", "pe", dr_marker[qid - 2])
                q = qid % 16
                s.wait("sy", *store_done[q])
                dr_loaded[qid] = (pn, s.op(
                    "sy", lambda e, qid=qid, q=q: e.dma_start(
                        out=drg[:, qid % 2, :, :], in_=dsp4[:, q, :, :]), inc=pn, n=16))

            load_w1(0)
            load_w1(1)
            s.wait("pe", "dv", g_ready_dv)
            s.wait("ac", "tbl", b0_cnt)
            for strip in range(NS):
                pn, v = w0_loaded[strip]
                s.wait("pe", pn, v)
                for ncl in range(4):
                    gnc = strip * 4 + ncl
                    slot = gnc % 8
                    half = slot // 4
                    banks = [(gnc * nth + t) % 4 for t in range(nth)]
                    for t in range(nth):
                        pe_wait_bank(banks[t])
                    for k in range(KC):
                        for t in range(nth):
                            mm(ps[:, banks[t], 0:th],
                               w0r[:, strip % 2, k, ncl * 128:(ncl + 1) * 128],
                               gfm[:, k, t * th:(t + 1) * th],
                               start=(k == 0), stop=(k == KC - 1),
                               inc="pe" if (k == KC - 1 and t == nth - 1) else None)
                    pm = s.cnt["pe"]
                    sm, v = dst_free[half]
                    s.wait("ac", sm, v)
                    s.wait("ac", "pe", pm)
                    for t in range(nth):
                        s.op("ac", lambda e, gnc=gnc, slot=slot, t=t, b=banks[t]: e.activation(
                            dst[:, slot, t * th:(t + 1) * th], ps[:, b, 0:th], AF.Gelu,
                            bias=b0sb[:, gnc:gnc + 1]),
                            inc="ac" if t == nth - 1 else None)
                    for t in range(nth):
                        bank_free[banks[t]] = ("ac", s.cnt["ac"])
                    if slot % 4 == 3:
                        ac_now = s.cnt["ac"]
                        s.wait("sy", "ac", ac_now)
                        bidx = gnc // 4
                        sname = "dsa" if half == 0 else "dsb"
                        s.op("sy", lambda e, bidx=bidx, half=half: e.dma_start(
                            out=dsp4[:, bidx, :, :],
                            in_=dst[:, half * 4:half * 4 + 4, :]), inc=sname, n=16)
                        dst_free[half] = (sname, s.cnt[sname])
                        store_done[bidx] = (sname, s.cnt[sname])
                        if bidx < 2:
                            ffn2_load_dr(bidx)
                w0_marker[strip] = s.cnt["pe"]
                if strip + 2 < NS:
                    load_w0(strip + 2)
            ffn1_done_pe = s.cnt["pe"]
            cut("FFN1")

            for hs in range(4):
                for n in range(64):
                    gid = hs * 8 + n // 8
                    qid = hs * 16 + n // 4
                    if n % 8 == 0:
                        pn, v = w1_loaded[gid]
                        s.wait("pe", pn, v)
                    if n % 4 == 0:
                        pn, v = dr_loaded[qid]
                        s.wait("pe", pn, v)
                    for tb in range(ntc):
                        if n == 0:
                            pe_wait_bank(tb % 8)
                        mm(ps[:, tb % 8, 0:512],
                           drg[:, qid % 2, n % 4, tb * 128:(tb + 1) * 128],
                           w1r[:, gid % 2, n % 8, :],
                           start=(n == 0), stop=False,
                           inc="pe" if (tb == ntc - 1 and n % 4 == 3) else None)
                    if n % 4 == 3:
                        dr_marker[qid] = s.cnt["pe"]
                        if 2 <= qid + 2 < 4 * 16:
                            ffn2_load_dr(qid + 2)
                    if n % 8 == 7:
                        w1_marker[gid] = s.cnt["pe"]
                        if gid + 2 < NW1:
                            load_w1(gid + 2)
                if hs == 0:
                    s.wait("pe", *ld_b1)
                for tb in range(ntc):
                    mm(ps[:, tb % 8, 0:512], ones_sb[0:1, :],
                       b1bf[0:1, hs * 512:(hs + 1) * 512],
                       start=False, stop=True,
                       inc="pe" if tb == ntc - 1 else None)
                pm = s.cnt["pe"]
                s.wait("dv", "pe", pm)
                half_n = max(1, ntc // 2)
                for tb in range(ntc):
                    last_of_half = (tb == half_n - 1) or (tb == ntc - 1)
                    s.op("dv", lambda e, tb=tb, hs=hs: e.tensor_tensor(
                        x1[:, tb, hs * 512:(hs + 1) * 512],
                        x1[:, tb, hs * 512:(hs + 1) * 512],
                        ps[:, tb % 8, 0:512], ALU.add),
                        inc="dv" if last_of_half else None)
                    if last_of_half:
                        for bb in range(tb + 1):
                            bank_free[bb % 8] = ("dv", s.cnt["dv"])
                pe_done = s.cnt["pe"]
                prev_strip_pe = pe_done
                s.wait("sy", "dv", s.cnt["dv"])
                s.op("sy", lambda e, hs=hs: e.dma_start(
                    out=y3s[:, :, hs * 512:(hs + 1) * 512],
                    in_=x1[:, :, hs * 512:(hs + 1) * 512]), inc="yst", n=16)

            s.wait("sy", "yst", s.cnt["yst"])

        except _Cut:
            pass
        _finish[0]()

    return nc


def _rope_tables(core, sloc):
    ntc = sloc // 128
    inv = 10000.0 ** (-(np.arange(0, 32, 2, dtype=np.float64)) / 32.0)
    pos = core * sloc + np.arange(sloc, dtype=np.float64)
    ang = (pos / 40.0)[:, None] * inv[None, :]
    cos = np.cos(ang).astype(np.float32).reshape(ntc, 128, 16).transpose(1, 0, 2)
    sin = np.sin(ang).astype(np.float32).reshape(ntc, 128, 16).transpose(1, 0, 2)
    return np.ascontiguousarray(cos), np.ascontiguousarray(sin)


def kernel(L0_attn_x, L0_attn_W_dkv, L0_attn_W_dq, L0_attn_W_uk, L0_attn_W_uv,
           L0_attn_W_uq, L0_attn_W_kr, L0_attn_W_qr, L0_attn_W_out,
           L0_ffn_W0, L0_ffn_bias0, L0_ffn_W1, L0_ffn_bias1):
    n_cores, sloc = 8, 1024
    x = np.ascontiguousarray(np.asarray(L0_attn_x, dtype=np.float32).reshape(8192, D))
    f32 = lambda a: np.ascontiguousarray(np.asarray(a, dtype=np.float32))
    w = dict(w_dkv=f32(L0_attn_W_dkv), w_dq=f32(L0_attn_W_dq), w_uk=f32(L0_attn_W_uk),
             w_uv=f32(L0_attn_W_uv), w_uq=f32(L0_attn_W_uq), w_kr=f32(L0_attn_W_kr),
             w_qr=f32(L0_attn_W_qr), w_out=f32(L0_attn_W_out),
             w0t=np.ascontiguousarray(np.asarray(L0_ffn_W0, dtype=np.float32).T),
             w1t=np.ascontiguousarray(np.asarray(L0_ffn_W1, dtype=np.float32).T),
             b0=f32(L0_ffn_bias0), b1=f32(L0_ffn_bias1),
             idm=np.eye(128, dtype=np.float32))
    in_maps = []
    for c in range(n_cores):
        cos, sin = _rope_tables(c, sloc)
        m = dict(w)
        m["x"] = np.ascontiguousarray(x[c * sloc:(c + 1) * sloc])
        m["cos"] = cos
        m["sin"] = sin
        in_maps.append(m)
    nc = build_nc(n_cores=n_cores, sloc=sloc)
    res = run_bass_kernel_spmd(nc, in_maps, core_ids=list(range(n_cores)))
    y = np.concatenate([np.asarray(r["y"]) for r in res.results], axis=0)
    return y.reshape(1, 8192, D).astype(np.float32)


# version 11
# speedup vs baseline: 1.0091x; 1.0091x over previous
import math
import numpy as np
import concourse.bass as bass
import concourse.mybir as mybir
from concourse.bass_utils import run_bass_kernel_spmd

F32 = mybir.dt.float32
BF16 = mybir.dt.bfloat16
F16 = mybir.dt.float16
U8 = mybir.dt.uint8
AF = mybir.ActivationFunctionType
ALU = mybir.AluOpType

D = 2048
CD = 1024
NH = 16
FF = 8192
KC = D // 128
CDC = CD // 128
ISQ = 1.0 / math.sqrt(128.0)
LN_EPS = 1e-5


def _hb(ap2d, nh):
    return bass.AP(ap2d.tensor, ap2d.offset,
                   [list(ap2d.ap[0]), [0, nh], list(ap2d.ap[1])])


class Sched:

    ENGS = ("pe", "dv", "ac", "sy", "gp")

    def __init__(self, nc):
        self.nc = nc
        self.q = {k: [] for k in self.ENGS}
        self.sems = {}
        self.cnt = {}

    def sem(self, name):
        self.sems[name] = self.nc.alloc_semaphore(name=name)
        self.cnt[name] = 0

    def wait(self, eng, sname, val=None):
        v = self.cnt[sname] if val is None else val
        if v <= 0:
            return
        h = self.sems[sname]
        self.q[eng].append(lambda e, h=h, v=v: e.wait_ge(h, v))

    def drain(self, eng):
        self.q[eng].append(lambda e: e.drain())

    def op(self, eng, fn, inc=None, n=1):
        if inc is None:
            self.q[eng].append(fn)
            return None
        self.cnt[inc] += n
        h = self.sems[inc]
        self.q[eng].append(lambda e, fn=fn, h=h, n=n: fn(e).then_inc(h, n))
        return self.cnt[inc]


def build_nc(n_cores=8, sloc=1024, upto=None):
    ntc = sloc // 128
    th = min(512, sloc)
    nth = sloc // th
    nc = bass.Bass("TRN2", target_bir_lowering=False)

    x_in = nc.dram_tensor("x", [sloc, D], F32, kind="ExternalInput")
    w_dkv = nc.dram_tensor("w_dkv", [D, CD], F32, kind="ExternalInput")
    w_dq = nc.dram_tensor("w_dq", [D, CD], F32, kind="ExternalInput")
    w_uk = nc.dram_tensor("w_uk", [CD, CD], F32, kind="ExternalInput")
    w_uv = nc.dram_tensor("w_uv", [CD, D], F32, kind="ExternalInput")
    w_uq = nc.dram_tensor("w_uq", [CD, CD], F32, kind="ExternalInput")
    w_kr = nc.dram_tensor("w_kr", [D, CD], F32, kind="ExternalInput")
    w_qr = nc.dram_tensor("w_qr", [CD, CD], F32, kind="ExternalInput")
    w_out = nc.dram_tensor("w_out", [D, D], F32, kind="ExternalInput")
    w0t = nc.dram_tensor("w0t", [D, FF], F32, kind="ExternalInput")
    w1t = nc.dram_tensor("w1t", [FF, D], F32, kind="ExternalInput")
    b0 = nc.dram_tensor("b0", [FF], F32, kind="ExternalInput")
    b1 = nc.dram_tensor("b1", [D], F32, kind="ExternalInput")
    cos_in = nc.dram_tensor("cos", [128, ntc, 16], F32, kind="ExternalInput")
    sin_in = nc.dram_tensor("sin", [128, ntc, 16], F32, kind="ExternalInput")
    id_in = nc.dram_tensor("idm", [128, 128], F32, kind="ExternalInput")
    y_out = nc.dram_tensor("y", [sloc, D], F32, kind="ExternalOutput")
    cc_in = nc.dram_tensor("cc_in", [128, NH, 128], F32)
    cc_out = nc.dram_tensor("cc_out", [128, NH, 128], F32)
    dsp = nc.dram_tensor("dsp", [FF, sloc], BF16)

    KB = 1024
    o_cos, o_sin = 0, 512
    o_id = 1024
    o_stats = 1280
    o_b0 = o_stats + 512
    o_ra = o_b0 + 256
    o_rb = o_ra + 512
    o_mst = o_rb + 512
    o_msbf = o_mst + 8 * KB
    o_ones = o_msbf + 4 * KB
    o_b1bf = o_ones + 256
    R0_END = o_b1bf + 4 * KB + 768
    o_hfm = R0_END
    o_ckv = o_hfm + 2 * KC * sloc
    o_cq = o_ckv + 2 * CDC * sloc
    o_krope = o_cq
    o_qc = o_cq + 2 * CDC * sloc
    o_krstg = o_qc
    o_qtm = o_qc + 2 * CDC * sloc
    o_xld = o_qtm + 2 * ntc * 16 * 128
    o_xsqA = o_xld + 16 * KB
    o_vtm = o_xld
    o_kboth = o_xld
    o_wsl = o_xld + max(24 * KB, 2 * ntc * 2048, 2 * 16 * sloc)
    TOTAL = o_wsl + 32 * KB
    o_w1r_x = None
    o_wkra_x = None
    if o_wsl - o_xld >= 32 * KB:
        o_wdq = o_xld
    else:
        o_wdq = None
    assert TOTAL <= 207 * KB, TOTAL
    if sloc >= 1024:
        o_apre = o_hfm
        o_x1 = o_ckv
        assert o_x1 + 4 * ntc * 2048 <= o_xld + 24 * KB
        o_gtm = o_x1 + 4 * ntc * 2048
        o_gfm = o_kboth
        o_wout = o_wsl
        o_xsqB = o_wsl + 16 * KB
        o_b1bc = o_wsl
        o_ef16 = o_wsl + 8 * KB
        o_w0r = o_apre
        o_w1r = o_apre
        o_dst = o_gtm
        o_drg = o_gtm
    else:
        cur = [TOTAL]

        def _al(szb):
            o = cur[0]
            cur[0] += (szb + 31) // 32 * 32
            return o

        o_apre = _al(2 * 16 * sloc)
        o_x1 = _al(4 * ntc * 2048)
        o_gtm = _al(16 * KB)
        o_gfm = _al(2 * 16 * sloc)
        o_shared = _al(32 * KB)
        o_wdq = o_shared
        o_wout = o_shared
        o_w0r = o_shared
        o_w1r = o_shared
        o_xsqB = _al(8 * KB)
        o_dst = _al(2 * 8 * sloc)
        o_drg = _al(2 * 8 * sloc)
        o_wuk_x = _al(16 * KB)
        o_woutb_x = o_wuk_x
        o_w1r_x = o_wuk_x
        o_wkra_x = _al(16 * KB)
        TOTAL = cur[0]
        assert TOTAL <= 207 * KB, TOTAL

    with (
        nc.sbuf_tensor("arena", [128, TOTAL], U8) as arena,
        nc.psum_tensor("ps", [128, 8, 512], F32) as ps,
        nc.Block() as block,
    ):
        base = nc.lookup_mloc(arena).addr
        ps_addr = nc.lookup_mloc(ps).addr
        pstr = nc.virtual_psum_tensor("pstr", [128, 2, 1024], BF16)
        nc._tensor(pstr.name, [128, 2, 1024], BF16, type="PSUM")
        _m = nc.lookup_mloc(pstr)
        _m.allocated = True
        _m.addr = ps_addr + 6 * 2048

        def sb(name, off, shape, dtype):
            h = nc.virtual_sbuf_tensor(name, shape, dtype)
            nc._tensor(h.name, list(shape), dtype, type="SB")
            mloc = nc.lookup_mloc(h)
            mloc.allocated = True
            mloc.addr = base + off
            return h

        cos_sb = sb("cos_sb", o_cos, [128, ntc, 16], F32)
        sin_sb = sb("sin_sb", o_sin, [128, ntc, 16], F32)
        ident = sb("ident", o_id, [128, 128], BF16)
        stats = sb("stats", o_stats, [128, 128], F32)
        b0sb = sb("b0sb", o_b0, [128, 64], F32)
        ropeA = sb("ropeA", o_ra, [128, 8, 16], F32)
        ropeB = sb("ropeB", o_rb, [128, 8, 16], F32)
        htmp = sb("htmp", o_gtm, [128, 4, 2048], BF16)
        mstage = sb("mstage", o_mst, [128, NH, 128], F32)
        msb = sb("msb", o_mst, [128, NH, 128], F32)
        msbf = sb("msbf", o_msbf, [128, NH, 128], BF16)
        hfm = sb("hfm", o_hfm, [128, KC, sloc], BF16)
        ckv = sb("ckv", o_ckv, [128, CDC, sloc], BF16)
        cq = sb("cq", o_cq, [128, CDC, sloc], BF16)
        qc = sb("qc", o_qc, [128, CDC, sloc], BF16)
        qtm = sb("qtm", o_qtm, [128, ntc, NH, 128], BF16)
        krope = sb("krope", o_krope, [128, ntc, NH, 64], BF16)
        krstg = sb("krstg", o_krstg, [64, NH, sloc], BF16)
        xld = sb("xld", o_xld, [128, 4, 2048], F32)
        vtm = sb("vtm", o_vtm, [128, ntc, 2048], BF16)
        kboth = sb("kboth", o_kboth, [128, NH, sloc], BF16)
        wsl = sb("wsl", o_wsl, [128, 16, 1024], BF16)
        wdq_sb = sb("wdq_sb", o_wdq, [128, 16, 1024], BF16)
        wkr_a = sb("wkr_a", (o_qc if sloc >= 1024 else o_wkra_x), [128, 16, 512], BF16)
        wuv_a = sb("wuv_a", o_wsl, [128, CDC, 1024], BF16)
        wuv_b = sb("wuv_b", o_wsl + 16 * KB, [128, CDC, 1024], BF16)
        ones_sb = sb("ones_sb", o_ones, [1, 128], BF16)
        b1bf = sb("b1bf", o_b1bf, [1, D], BF16)
        wuq_sb = sb("wuq_sb", o_wsl, [128, CDC, 1024], BF16)
        wqr_sb = sb("wqr_sb", o_wsl + 16 * KB, [128, CDC, 1024], BF16)
        wuv_sb = sb("wuv_sb", o_wsl, [128, CDC, 2048], BF16)
        wuk_sb = sb("wuk_sb", (o_gtm if sloc >= 1024 else o_wuk_x), [128, CDC, 1024], BF16)
        apre = sb("apre", o_apre, [128, NH, sloc], BF16)
        x1 = sb("x1", o_x1, [128, ntc, 2048], F32)
        gtm = sb("gtm", o_gtm, [128, 4, 2048], BF16)
        gfm = sb("gfm", o_gfm, [128, KC, sloc], BF16)
        wouts = sb("wouts", o_wout, [128, 16, 512], BF16)
        wouts_b = sb("wouts_b", (o_gtm if sloc >= 1024 else o_woutb_x), [128, 16, 512], BF16)
        xsqB = sb("xsqB", o_xsqB, [128, 2048], F32)
        w0r = sb("w0r", o_w0r, [128, 2, 16, 512], BF16)
        w1r = sb("w1r", (o_wsl + 16 * KB if sloc >= 1024 else o_w1r_x), [128, 2, 8, 512], BF16)
        dst = sb("dst", o_dst, [128, 8, sloc], BF16)
        drg = sb("drg", (o_wsl if sloc >= 1024 else o_drg), [128, 2, 4, sloc], BF16)


        s = Sched(nc)
        for nm in ("pe", "dv", "ac", "gw0", "gw1", "gw2", "gw3", "idl", "tbl",
                   "xa", "xb", "cc", "cci", "msb", "krs", "x1i", "dsa", "dsb",
                   "w0a", "w0b", "w1a", "w1b", "dra", "drb", "yst", "b1l"):
            s.sem(nm)
        _gwi = [0]

        def wload(fn):
            pn = "gw%d" % (_gwi[0] % 4)
            _gwi[0] += 1
            return (pn, s.op("gp", fn, inc=pn, n=16))

        _finish = [None]

        def _mk_finish():
            def fin():
                @block.tensor
                def _(t):
                    for f in s.q["pe"]:
                        f(t)

                @block.vector
                def _(v):
                    for f in s.q["dv"]:
                        f(v)

                @block.scalar
                def _(a):
                    for f in s.q["ac"]:
                        f(a)

                @block.sync
                def _(sy):
                    for f in s.q["sy"]:
                        f(sy)

                @block.gpsimd
                def _(g):
                    for f in s.q["gp"]:
                        f(g)
            return fin
        _finish[0] = _mk_finish()

        class _Cut(Exception):
            pass

        def cut(tag):
            if upto == tag:
                raise _Cut()

        bank_free = [("dv", 0)] * 8
        slot_free = [("dv", 0)] * 8
        EPS_AP = stats[:, 120:121]
        try:

            def pe_wait_bank(b):
                sm, v = bank_free[b]
                s.wait("pe", sm, v)

            def mm(psum_ap, lhsT, rhs, start, stop, inc=None):
                return s.op("pe", lambda e, a=psum_ap, l=lhsT, r=rhs, st=start,
                            sp=stop: e.matmul(a, l, r, start=st, stop=sp), inc=inc)

            tbank_free = {6: ("dv", 0), 7: ("dv", 0)}

            def transpose_groups(n_tiles, src_fn, dst_fn, rows=128, upper=False):
                p0 = 64 if upper else 0
                tp = (0, 64) if upper else None
                ngr = (n_tiles + 7) // 8
                for g in range(ngr):
                    bk = 6 + (g % 2)
                    area = pstr[:, g % 2, :]
                    sm, v = tbank_free[bk]
                    s.wait("pe", sm, v)
                    lo = g * 8
                    hi = min(lo + 8, n_tiles)
                    for i in range(lo, hi):
                        sl = i - lo
                        s.op("pe", lambda e, i=i, sl=sl, area=area: e.transpose(
                            area[p0:p0 + rows, sl * 128:(sl + 1) * 128], src_fn(i),
                            ident[:, :], tile_position=tp),
                            inc="pe" if i == hi - 1 else None)
                    pm = s.cnt["pe"]
                    s.wait("dv", "pe", pm)
                    for i in range(lo, hi):
                        sl = i - lo
                        s.op("dv", lambda e, i=i, sl=sl, area=area: e.tensor_copy(
                            dst_fn(i), area[p0:p0 + rows, sl * 128:(sl + 1) * 128]),
                            inc="dv" if i == hi - 1 else None)
                    tbank_free[bk] = ("dv", s.cnt["dv"])

            s.op("dv", lambda e: e.memset(stats[:, 120:121], LN_EPS), inc="dv")
            s.op("gp", lambda e: e.dma_start(out=ident[:, :], in_=id_in[:, :]), inc="idl", n=16)
            s.op("sy", lambda e: e.dma_start(out=cos_sb[:, :, :], in_=cos_in[:, :, :]), inc="tbl", n=16)
            s.op("sy", lambda e: e.dma_start(out=sin_sb[:, :, :], in_=sin_in[:, :, :]), inc="tbl", n=16)
            tbl_all = s.cnt["tbl"]
            s.op("dv", lambda e: e.memset(ones_sb[0:1, :], 1.0))
            cut("A0")

            x3 = x_in.ap().rearrange("(tc p) d -> p tc d", p=128)
            htmp_free = [("pe", 0)] * 4
            SP, SQ, SM, SI, SV, SS2 = 0, 8, 16, 24, 32, 40
            p1_rd = {}
            for tc in range(ntc):
                par = tc % 4 if ntc >= 4 else tc % 2
                qeng, qsem = ("sy", "xa") if tc % 2 == 0 else ("ac", "xb")
                dep = 4 if ntc >= 4 else 2
                if tc >= dep:
                    a_c, d_c = p1_rd[tc - dep]
                    s.wait(qeng, "ac", a_c)
                    s.wait(qeng, "dv", d_c)
                xw = s.op(qeng, lambda e, t=tc, p=par: e.dma_start(
                    out=xld[:, p, :], in_=x3[:, t, :]), inc=qsem, n=16)
                s.wait("ac", qsem, xw)
                s.drain("ac")
                s.op("ac", lambda e, p=par, t=tc: e.activation(
                    htmp[:, p % 4 if ntc >= 4 else p % 2, :], xld[:, p, :], AF.Square,
                    accum_out=stats[:, SQ + t:SQ + t + 1]), inc="ac")
                sq_done = s.cnt["ac"]
                s.wait("dv", qsem, xw)
                s.op("dv", lambda e, p=par, t=tc: e.reduce_sum(
                    stats[:, SP + t:SP + t + 1], xld[:, p, :],
                    axis=mybir.AxisListType.X), inc="dv")
                p1_rd[tc] = (sq_done, s.cnt["dv"])
            s.wait("gp", "xa", s.cnt["xa"])
            ld_dkv = wload(lambda e: e.dma_start(
                out=wsl[:, :, :], in_=w_dkv.ap().rearrange("(k p) m -> p k m", p=128)))
            s.wait("dv", "ac", s.cnt["ac"])
            s.drain("dv")
            for tc in range(ntc):
                s.op("dv", lambda e, t=tc: e.tensor_single_scalar(
                    stats[:, SM + t:SM + t + 1], stats[:, SP + t:SP + t + 1],
                    1.0 / D, ALU.mult))
                s.op("dv", lambda e, t=tc: e.tensor_single_scalar(
                    stats[:, SS2 + t:SS2 + t + 1], stats[:, SQ + t:SQ + t + 1],
                    1.0 / D, ALU.mult))
            s.drain("dv")
            for tc in range(ntc):
                s.op("dv", lambda e, t=tc: e.tensor_tensor(
                    stats[:, SV + t:SV + t + 1], stats[:, SM + t:SM + t + 1],
                    stats[:, SM + t:SM + t + 1], ALU.mult))
            s.drain("dv")
            s.op("dv", lambda e: e.tensor_tensor(
                stats[:, SV:SV + ntc], stats[:, SS2:SS2 + ntc],
                stats[:, SV:SV + ntc], ALU.subtract), inc="dv")
            s.wait("ac", "dv", s.cnt["dv"])
            s.op("ac", lambda e: e.activation(
                stats[:, SS2:SS2 + ntc], stats[:, SV:SV + ntc], AF.Sqrt,
                bias=EPS_AP), inc="ac")
            s.wait("dv", "ac", s.cnt["ac"])
            s.op("dv", lambda e: e.reciprocal(
                stats[:, SI:SI + ntc], stats[:, SS2:SS2 + ntc]), inc="dv")
            s.drain("dv")
            inv_ready_dv = s.cnt["dv"]
            p1_ac_all = s.cnt["ac"]
            p2_norm = {}
            order = (list(range(4, ntc)) + list(range(4))) if ntc >= 8 else list(range(ntc))
            first = True
            for tc in order:
                par = tc % 4 if ntc >= 4 else tc % 2
                hp = par if ntc >= 4 else par % 2
                resident = ntc >= 8 and tc >= 4
                if resident:
                    xw = None
                else:
                    qeng, qsem = ("sy", "xa") if tc % 2 == 0 else ("ac", "xb")
                    if ntc >= 8:
                        s.wait(qeng, "dv", p2_norm.get(tc + 4, inv_ready_dv))
                        s.wait(qeng, "ac", p1_ac_all)
                    elif tc >= 2:
                        s.wait(qeng, "dv", p2_norm[tc - 2])
                    else:
                        s.wait(qeng, "dv", inv_ready_dv)
                        s.wait(qeng, "ac", p1_ac_all)
                    xw = s.op(qeng, lambda e, t=tc, p=par: e.dma_start(
                        out=xld[:, p, :], in_=x3[:, t, :]), inc=qsem, n=16)
                sm, v = htmp_free[hp]
                s.wait("dv", sm, v)
                if resident:
                    s.wait("dv", "dv", inv_ready_dv)
                    s.wait("dv", "ac", p1_ac_all)
                else:
                    s.wait("dv", qsem, xw)
                s.op("dv", lambda e, p=par, hp=hp, t=tc: e.tensor_scalar(
                    htmp[:, hp, :], xld[:, p, :], stats[:, SM + t:SM + t + 1],
                    stats[:, SI + t:SI + t + 1], ALU.subtract, ALU.mult), inc="dv")
                h_done = s.cnt["dv"]
                p2_norm[tc] = h_done
                s.wait("pe", "dv", h_done)
                if first:
                    s.wait("pe", "idl", 16)
                    first = False
                transpose_groups(
                    KC,
                    lambda i, hp=hp: htmp[:, hp, i * 128:(i + 1) * 128],
                    lambda i, t=tc: hfm[:, i, t * 128:(t + 1) * 128])
                htmp_free[hp] = ("pe", s.cnt["pe"])
            h_ready = s.cnt["dv"]
            s.op("sy", lambda e: e.dma_start(
                out=b0sb[:, :], in_=b0.ap().rearrange("(c p) -> p c", p=128),
                allow_slow_non_contiguous=True), inc="tbl", n=16)
            b0_cnt = s.cnt["tbl"]
            cut("A1")

            def proj_fm(dstT, w_sb, n_out_chunks, n_k, rhs_sb):
                for m in range(n_out_chunks):
                    for t in range(nth):
                        b = (m * nth + t) % 4
                        pe_wait_bank(b)
                        for k in range(n_k):
                            mm(ps[:, b, 0:th],
                               w_sb[:, k, m * 128:(m + 1) * 128],
                               rhs_sb[:, k, t * th:(t + 1) * th],
                               start=(k == 0), stop=(k == n_k - 1),
                               inc="pe" if k == n_k - 1 else None)
                        pm = s.cnt["pe"]
                        s.wait("dv", "pe", pm)
                        s.op("dv", lambda e, m=m, t=t, b=b: e.tensor_copy(
                            dstT[:, m, t * th:(t + 1) * th], ps[:, b, 0:th]), inc="dv")
                        bank_free[b] = ("dv", s.cnt["dv"])

            s.wait("gp", "dv", h_ready)
            s.wait("gp", "ac", s.cnt["ac"])
            ld_dq = wload(lambda e: e.dma_start(
                out=wdq_sb[:, :, :], in_=w_dq.ap().rearrange("(k p) m -> p k m", p=128)))
            s.wait("pe", *ld_dkv)
            s.wait("pe", "dv", h_ready)
            proj_fm(ckv, wsl, CDC, KC, hfm)
            ckv_done_pe = s.cnt["pe"]
            cut("A2")

            s.wait("gp", "pe", ckv_done_pe)
            ld_uq = wload(lambda e: e.dma_start(
                out=wuq_sb[:, :, :], in_=w_uq.ap().rearrange("(k p) m -> p k m", p=128)))
            ld_qr = wload(lambda e: e.dma_start(
                out=wqr_sb[:, :, :], in_=w_qr.ap().rearrange("(k p) m -> p k m", p=128)))
            s.wait("pe", *ld_dq)
            proj_fm(cq, wdq_sb, CDC, KC, hfm)
            cq_done_pe = s.cnt["pe"]

            s.wait("pe", *ld_uq)
            s.wait("pe", "dv")
            proj_fm(qc, wuq_sb, CDC, CDC, cq)
            cut("A4")

            for tc in range(ntc):
                for g in range(2):
                    b = (tc * 2 + g) % 4
                    pe_wait_bank(b)
                    for k in range(CDC):
                        mm(ps[:, b, 0:512], cq[:, k, tc * 128:(tc + 1) * 128],
                           wuq_sb[:, k, g * 512:(g + 1) * 512],
                           start=(k == 0), stop=(k == CDC - 1),
                           inc="pe" if k == CDC - 1 else None)
                    pm = s.cnt["pe"]
                    s.wait("dv", "pe", pm)
                    s.op("dv", lambda e, tc=tc, g=g, b=b: e.tensor_copy(
                        qtm[:, tc, 8 * g:8 * g + 8, 0:64],
                        ps[:, b, 0:512].rearrange("p (h c) -> p h c", h=8)), inc="dv")
                    bank_free[b] = ("dv", s.cnt["dv"])
            a5_done_pe = s.cnt["pe"]
            s.wait("gp", "pe", a5_done_pe)
            ld_uva = wload(lambda e: e.dma_start(
                out=wuv_a[:, :, :],
                in_=w_uv.ap().rearrange("(k p) m -> p k m", p=128)[:, :, 0:1024]))

            def rope_block(psum_ap, out_ap, tc):
                P = psum_ap.rearrange("p (h c) -> p h c", h=8)
                cl = _hb(cos_sb[:, tc, :], 8)
                sl_ = _hb(sin_sb[:, tc, :], 8)
                s.drain("dv")
                s.op("dv", lambda e, P=P, cl=cl: e.tensor_tensor(
                    ropeA[:, :, :], P[:, :, 0:16], cl, ALU.mult))
                s.op("dv", lambda e, P=P, sl_=sl_: e.tensor_tensor(
                    ropeB[:, :, :], P[:, :, 16:32], sl_, ALU.mult))
                s.drain("dv")
                s.op("dv", lambda e, out_ap=out_ap: e.tensor_tensor(
                    out_ap[:, :, 0:16], ropeA[:, :, :], ropeB[:, :, :], ALU.subtract))
                s.drain("dv")
                s.op("dv", lambda e, P=P, cl=cl: e.tensor_tensor(
                    ropeA[:, :, :], P[:, :, 16:32], cl, ALU.mult))
                s.op("dv", lambda e, P=P, sl_=sl_: e.tensor_tensor(
                    ropeB[:, :, :], P[:, :, 0:16], sl_, ALU.mult))
                s.drain("dv")
                s.op("dv", lambda e, out_ap=out_ap: e.tensor_tensor(
                    out_ap[:, :, 16:32], ropeA[:, :, :], ropeB[:, :, :], ALU.add))
                s.op("dv", lambda e, P=P, out_ap=out_ap: e.tensor_copy(
                    out_ap[:, :, 32:64], P[:, :, 32:64]), inc="dv")

            s.wait("dv", "tbl", b0_cnt)
            s.wait("pe", *ld_qr)
            s.wait("pe", "dv")
            for tc in range(ntc):
                for g in range(2):
                    b = (tc * 2 + g) % 4
                    pe_wait_bank(b)
                    for k in range(CDC):
                        mm(ps[:, b, 0:512], qc[:, k, tc * 128:(tc + 1) * 128],
                           wqr_sb[:, k, g * 512:(g + 1) * 512],
                           start=(k == 0), stop=(k == CDC - 1),
                           inc="pe" if k == CDC - 1 else None)
                    pm = s.cnt["pe"]
                    s.wait("dv", "pe", pm)
                    rope_block(ps[:, b, 0:512], qtm[:, tc, 8 * g:8 * g + 8, 64:128], tc)
                    bank_free[b] = ("dv", s.cnt["dv"])
            qr_done_pe = s.cnt["pe"]
            s.wait("gp", "pe", qr_done_pe)
            ld_kra = wload(lambda e: e.dma_start(
                out=wkr_a[:, :, :],
                in_=w_kr.ap().rearrange("(k p) m -> p k m", p=128)[:, :, 0:512]))
            cut("A6")

            s.wait("gp", "pe", qr_done_pe)
            ld_uvb = wload(lambda e: e.dma_start(
                out=wuv_b[:, :, :],
                in_=w_uv.ap().rearrange("(k p) m -> p k m", p=128)[:, :, 1024:2048]))
            v3 = vtm.ap().rearrange("p tc (vc c) -> p tc vc c", c=512)
            for vc in range(4):
                s.wait("pe", *(ld_uva if vc < 2 else ld_uvb))
                wv = wuv_a if vc < 2 else wuv_b
                vcl = vc % 2
                for tc in range(ntc):
                    b = (vc * ntc + tc) % 4
                    pe_wait_bank(b)
                    for k in range(CDC):
                        mm(ps[:, b, 0:512], ckv[:, k, tc * 128:(tc + 1) * 128],
                           wv[:, k, vcl * 512:(vcl + 1) * 512],
                           start=(k == 0), stop=(k == CDC - 1),
                           inc="pe" if k == CDC - 1 else None)
                    pm = s.cnt["pe"]
                    s.wait("dv", "pe", pm)
                    s.op("dv", lambda e, tc=tc, vc=vc, b=b: e.tensor_copy(
                        v3[:, tc, vc, :], ps[:, b, 0:512]), inc="dv")
                    bank_free[b] = ("dv", s.cnt["dv"])
            v_done_pe = s.cnt["pe"]
            cut("A7")

            s.wait("pe", "dv", s.cnt["dv"])
            for hp in range(NH // 2):
                pe_wait_bank(4)
                pe_wait_bank(5)
                for j in range(2):
                    h = 2 * hp + j
                    for tc in range(ntc):
                        mm(ps[:, 4 + j, 0:128], qtm[:, tc, h, :],
                           vtm[:, tc, h * 128:(h + 1) * 128],
                           start=(tc == 0), stop=(tc == ntc - 1),
                           inc="pe" if (j == 1 and tc == ntc - 1) else None)
                pm = s.cnt["pe"]
                s.wait("dv", "pe", pm)
                for j in range(2):
                    h = 2 * hp + j
                    s.op("dv", lambda e, h=h, j=j: e.tensor_single_scalar(
                        mstage[:, h, :], ps[:, 4 + j, 0:128], ISQ, ALU.mult),
                        inc="dv" if j == 1 else None)
                bank_free[4] = ("dv", s.cnt["dv"])
                bank_free[5] = ("dv", s.cnt["dv"])
            m_evac_done = s.cnt["dv"]
            cut("A8")

            m_done_pe = s.cnt["pe"]
            s.wait("sy", "dv", m_evac_done)
            s.op("sy", lambda e: e.dma_start(out=cc_in[:, :, :], in_=mstage[:, :, :]),
                 inc="cci", n=16)
            s.wait("gp", "pe", v_done_pe)
            ld_krb = wload(lambda e: e.dma_start(
                out=wsl[:, :, 0:512],
                in_=w_kr.ap().rearrange("(k p) m -> p k m", p=128)[:, :, 512:1024]))
            s.wait("gp", "cci", 16)
            s.op("gp", lambda e: e.collective_compute(
                "AllReduce", ALU.add, replica_groups=[list(range(n_cores))],
                ins=[cc_in.ap().opt()], outs=[cc_out.ap().opt()]), inc="cc")
            s.wait("sy", "cc", 1)
            s.op("sy", lambda e: e.dma_start(out=msb[:, :, :], in_=cc_out[:, :, :]),
                 inc="msb", n=16)
            s.wait("gp", "pe", m_done_pe)
            ld_uk = wload(lambda e: e.dma_start(
                out=wuk_sb[:, :, :], in_=w_uk.ap().rearrange("(k p) m -> p k m", p=128)))
            ld_b1 = ("b1l", s.op("gp", lambda e: e.dma_start(
                out=b1bf[0:1, :], in_=b1.ap().unsqueeze(0)), inc="b1l", n=16))

            for g in range(2):
                s.wait("pe", *(ld_kra if g == 0 else ld_krb))
                wk = wkr_a if g == 0 else wsl
                for tc in range(ntc):
                    b = (g * ntc + tc) % 4
                    pe_wait_bank(b)
                    for k in range(KC):
                        mm(ps[:, b, 0:512], hfm[:, k, tc * 128:(tc + 1) * 128],
                           wk[:, k, 0:512],
                           start=(k == 0), stop=(k == KC - 1),
                           inc="pe" if k == KC - 1 else None)
                    pm = s.cnt["pe"]
                    s.wait("dv", "pe", pm)
                    rope_block(ps[:, b, 0:512], krope[:, tc, 8 * g:8 * g + 8, 0:64], tc)
                    bank_free[b] = ("dv", s.cnt["dv"])
            kr_done_pe = s.cnt["pe"]
            krope_done_dv = s.cnt["dv"]
            cut("A11")

            s.wait("pe", "dv", krope_done_dv)
            transpose_groups(
                ntc * NH,
                lambda i: krope[:, i // NH, i % NH, :],
                lambda i: kboth[64:128, i % NH, (i // NH) * 128:(i // NH) * 128 + 128],
                rows=64, upper=True)
            krstg_done_dv = s.cnt["dv"]
            krT_done_pe = s.cnt["pe"]
            cut("A12")

            s.wait("pe", *ld_uk)
            for h in range(NH):
                for t in range(nth):
                    b = (h * nth + t) % 4
                    pe_wait_bank(b)
                    for k in range(CDC):
                        mm(ps[0:64, b, 0:th], wuk_sb[:, k, h * 64:(h + 1) * 64],
                           ckv[:, k, t * th:(t + 1) * th],
                           start=(k == 0), stop=(k == CDC - 1),
                           inc="pe" if k == CDC - 1 else None)
                    pm = s.cnt["pe"]
                    s.wait("ac", "pe", pm)
                    s.op("ac", lambda e, h=h, t=t, b=b: e.copy(
                        kboth[0:64, h, t * th:(t + 1) * th], ps[0:64, b, 0:th]), inc="ac")
                    bank_free[b] = ("ac", s.cnt["ac"])
            kc_done_pe = s.cnt["pe"]
            kc_done_ac = s.cnt["ac"]
            cut("A10")

            s.wait("sy", "dv", krstg_done_dv)
            s.wait("sy", "pe", kc_done_pe)
            s.op("sy", lambda e: e.dma_start(out=x1[:, :, :], in_=x3[:, :, :]),
                 inc="x1i", n=16)

            s.wait("dv", "msb", 16)
            s.op("dv", lambda e: e.tensor_copy(msbf[:, :, :], msb[:, :, :]), inc="dv")
            msbf_done = s.cnt["dv"]
            cut("AR")

            s.wait("pe", "dv", msbf_done)
            units = [(h, t) for h in range(NH) for t in range(nth)]
            for g0 in range(0, len(units), 8):
                grp = units[g0:g0 + 8]
                for b in range(min(8, len(grp))):
                    pe_wait_bank(b)
                for i, (h, t) in enumerate(grp):
                    mm(ps[:, i, 0:th], msbf[:, h, :], kboth[:, h, t * th:(t + 1) * th],
                       start=True, stop=True,
                       inc="pe" if i == len(grp) - 1 else None)
                pm = s.cnt["pe"]
                s.wait("dv", "pe", pm)
                s.wait("ac", "pe", pm)
                for i, (h, t) in enumerate(grp):
                    eng = "dv" if i % 2 == 0 else "ac"
                    fn = (lambda e, h=h, t=t, i=i: e.tensor_copy(
                        apre[:, h, t * th:(t + 1) * th], ps[:, i, 0:th])) \
                        if eng == "dv" else \
                        (lambda e, h=h, t=t, i=i: e.copy(
                            apre[:, h, t * th:(t + 1) * th], ps[:, i, 0:th]))
                    s.op(eng, fn, inc=eng if i >= len(grp) - 2 else None)
                for b in range(min(8, len(grp))):
                    bank_free[b] = ("dv", s.cnt["dv"]) if b % 2 == 0 else ("ac", s.cnt["ac"])

            s.wait("pe", "dv")
            s.wait("pe", "ac")
            wo4 = w_out.ap().rearrange("(m p) o -> p m o", p=128)
            wout_marker = {-2: kr_done_pe, -1: kc_done_pe}
            wout_loaded = {}

            def load_wout(og):
                s.wait("gp", "pe", wout_marker[og - 2])
                buf = wouts if og % 2 == 0 else wouts_b
                wout_loaded[og] = wload(lambda e, og=og, buf=buf: e.dma_start(
                    out=buf[:, :, :], in_=wo4[:, :, og * 512:(og + 1) * 512]))

            load_wout(0)
            load_wout(1)
            s.wait("dv", "x1i", 16)
            for og in range(4):
                s.wait("pe", *wout_loaded[og])
                wbuf = wouts if og % 2 == 0 else wouts_b
                for tc in range(ntc):
                    b = (og * ntc + tc) % 4
                    pe_wait_bank(b)
                    for m in range(16):
                        mm(ps[:, b, 0:512], apre[:, m, tc * 128:(tc + 1) * 128],
                           wbuf[:, m, :],
                           start=(m == 0), stop=(m == 15),
                           inc="pe" if m == 15 else None)
                    pm = s.cnt["pe"]
                    s.wait("dv", "pe", pm)
                    s.op("dv", lambda e, tc=tc, og=og, b=b: e.tensor_tensor(
                        x1[:, tc, og * 512:(og + 1) * 512],
                        x1[:, tc, og * 512:(og + 1) * 512],
                        ps[:, b, 0:512], ALU.add), inc="dv")
                    add_done = s.cnt["dv"]
                    bank_free[b] = ("dv", add_done)
                    s.drain("dv")
                    s.op("dv", lambda e, tc=tc, og=og: e.reduce_sum(
                        stats[:, 48 + tc * 4 + og:48 + tc * 4 + og + 1],
                        x1[:, tc, og * 512:(og + 1) * 512],
                        axis=mybir.AxisListType.X), inc="dv")
                    s.wait("ac", "dv", add_done)
                    s.drain("ac")
                    s.op("ac", lambda e, tc=tc, og=og: e.activation(
                        xsqB[:, 0:512], x1[:, tc, og * 512:(og + 1) * 512],
                        AF.Square,
                        accum_out=stats[:, 80 + tc * 4 + og:80 + tc * 4 + og + 1]),
                        inc="ac")
                wout_marker[og] = s.cnt["pe"]
                if og + 2 < 4:
                    load_wout(og + 2)
            b12_done_pe_v = s.cnt["pe"]
            attn_done_dv = s.cnt["dv"]
            cut("B12")

            ln2_parts_dv = s.cnt["dv"]
            ln2_parts_ac = s.cnt["ac"]
            s.drain("dv")
            for tc in range(ntc):
                s.op("dv", lambda e, t=tc: e.reduce_sum(
                    stats[:, SP + t:SP + t + 1], stats[:, 48 + t * 4:48 + t * 4 + 4],
                    axis=mybir.AxisListType.X))
            s.wait("dv", "ac", ln2_parts_ac)
            for tc in range(ntc):
                s.op("dv", lambda e, t=tc: e.reduce_sum(
                    stats[:, SQ + t:SQ + t + 1], stats[:, 80 + t * 4:80 + t * 4 + 4],
                    axis=mybir.AxisListType.X))
            s.drain("dv")
            for tc in range(ntc):
                s.op("dv", lambda e, t=tc: e.tensor_single_scalar(
                    stats[:, SM + t:SM + t + 1], stats[:, SP + t:SP + t + 1],
                    1.0 / D, ALU.mult))
                s.op("dv", lambda e, t=tc: e.tensor_single_scalar(
                    stats[:, SS2 + t:SS2 + t + 1], stats[:, SQ + t:SQ + t + 1],
                    1.0 / D, ALU.mult))
            s.drain("dv")
            for tc in range(ntc):
                s.op("dv", lambda e, t=tc: e.tensor_tensor(
                    stats[:, SV + t:SV + t + 1], stats[:, SM + t:SM + t + 1],
                    stats[:, SM + t:SM + t + 1], ALU.mult))
            s.drain("dv")
            s.op("dv", lambda e: e.tensor_tensor(
                stats[:, SV:SV + ntc], stats[:, SS2:SS2 + ntc],
                stats[:, SV:SV + ntc], ALU.subtract), inc="dv")
            s.wait("ac", "dv", s.cnt["dv"])
            s.op("ac", lambda e: e.activation(
                stats[:, SS2:SS2 + ntc], stats[:, SV:SV + ntc], AF.Sqrt,
                bias=EPS_AP), inc="ac")
            s.wait("dv", "ac", s.cnt["ac"])
            s.op("dv", lambda e: e.reciprocal(
                stats[:, SI:SI + ntc], stats[:, SS2:SS2 + ntc]), inc="dv")
            s.drain("dv")
            gtm_free = [("pe", 0)] * 4
            for tc in range(ntc):
                par = tc % 4 if ntc >= 4 else tc % 2
                sm, v = gtm_free[par]
                s.wait("dv", sm, v)
                s.op("dv", lambda e, t=tc, p=par: e.tensor_scalar(
                    gtm[:, p, :], x1[:, t, :], stats[:, SM + t:SM + t + 1],
                    stats[:, SI + t:SI + t + 1], ALU.subtract, ALU.mult), inc="dv")
                g_done = s.cnt["dv"]
                s.wait("pe", "dv", g_done)
                transpose_groups(
                    KC,
                    lambda i, p=par: gtm[:, p, i * 128:(i + 1) * 128],
                    lambda i, t=tc: gfm[:, i, t * 128:(t + 1) * 128])
                gtm_free[par] = ("pe", s.cnt["pe"])
            g_ready_dv = s.cnt["dv"]
            b13_pe = s.cnt["pe"]
            b13_ac = s.cnt["ac"]
            cut("B13")

            w0v = w0t.ap().rearrange("(k p) n -> p k n", p=128)
            NS = FF // 512
            b12_done_pe = b12_done_pe_v
            w0_marker = {-2: b12_done_pe, -1: b12_done_pe}
            dst_free = [("ac", 0), ("ac", 0)]
            dsp4 = dsp.ap().rearrange("(b n4 p) t -> p b n4 t", p=128, n4=4)

            w0_loaded = {}
            store_done = {}

            def load_w0(strip):
                pn = "w0a" if strip % 2 == 0 else "w0b"
                s.wait("gp", "pe", w0_marker[strip - 2])
                w0_loaded[strip] = (pn, s.op(
                    "gp", lambda e, st=strip, p=strip % 2: e.dma_start(
                        out=w0r[:, p, :, :], in_=w0v[:, :, st * 512:(st + 1) * 512]),
                    inc=pn, n=16))

            load_w0(0)
            load_w0(1)
            y3s = y_out.ap().rearrange("(tc p) d -> p tc d", p=128)
            w1v = w1t.ap().rearrange("(g8 n2 p) h -> p g8 n2 h", p=128, n2=8)
            NW1 = 4 * (FF // 1024)
            w1_marker = {-2: b13_pe, -1: b13_pe}
            dr_marker = {-2: b13_pe, -1: b13_pe}
            w1_loaded = {}
            dr_loaded = {}

            def load_w1(gid):
                pn = "w1a" if gid % 2 == 0 else "w1b"
                s.wait("gp", "pe", w1_marker[gid - 2])
                if gid < 2:
                    s.wait("gp", "ac", b13_ac)
                w1_loaded[gid] = (pn, s.op(
                    "gp", lambda e, gid=gid: e.dma_start(
                        out=w1r[:, gid % 2, :, :],
                        in_=w1v[:, gid % 8, :, (gid // 8) * 512:(gid // 8) * 512 + 512]),
                    inc=pn, n=16))

            def ffn2_load_dr(qid):
                pn = "dra" if qid % 2 == 0 else "drb"
                s.wait("sy", "pe", dr_marker[qid - 2])
                q = qid % 16
                s.wait("sy", *store_done[q])
                dr_loaded[qid] = (pn, s.op(
                    "sy", lambda e, qid=qid, q=q: e.dma_start(
                        out=drg[:, qid % 2, :, :], in_=dsp4[:, q, :, :]), inc=pn, n=16))

            load_w1(0)
            load_w1(1)
            s.wait("pe", "dv", g_ready_dv)
            s.wait("ac", "tbl", b0_cnt)
            for strip in range(NS):
                pn, v = w0_loaded[strip]
                s.wait("pe", pn, v)
                for ncl in range(4):
                    gnc = strip * 4 + ncl
                    slot = gnc % 8
                    half = slot // 4
                    banks = [(gnc * nth + t) % 4 for t in range(nth)]
                    for t in range(nth):
                        pe_wait_bank(banks[t])
                    for k in range(KC):
                        for t in range(nth):
                            mm(ps[:, banks[t], 0:th],
                               w0r[:, strip % 2, k, ncl * 128:(ncl + 1) * 128],
                               gfm[:, k, t * th:(t + 1) * th],
                               start=(k == 0), stop=(k == KC - 1),
                               inc="pe" if (k == KC - 1 and t == nth - 1) else None)
                    pm = s.cnt["pe"]
                    sm, v = dst_free[half]
                    s.wait("ac", sm, v)
                    s.wait("ac", "pe", pm)
                    for t in range(nth):
                        s.op("ac", lambda e, gnc=gnc, slot=slot, t=t, b=banks[t]: e.activation(
                            dst[:, slot, t * th:(t + 1) * th], ps[:, b, 0:th], AF.Gelu,
                            bias=b0sb[:, gnc:gnc + 1]),
                            inc="ac" if t == nth - 1 else None)
                    for t in range(nth):
                        bank_free[banks[t]] = ("ac", s.cnt["ac"])
                    if slot % 4 == 3:
                        ac_now = s.cnt["ac"]
                        s.wait("sy", "ac", ac_now)
                        bidx = gnc // 4
                        sname = "dsa" if half == 0 else "dsb"
                        s.op("sy", lambda e, bidx=bidx, half=half: e.dma_start(
                            out=dsp4[:, bidx, :, :],
                            in_=dst[:, half * 4:half * 4 + 4, :]), inc=sname, n=16)
                        dst_free[half] = (sname, s.cnt[sname])
                        store_done[bidx] = (sname, s.cnt[sname])
                        if bidx < 2:
                            ffn2_load_dr(bidx)
                w0_marker[strip] = s.cnt["pe"]
                if strip + 2 < NS:
                    load_w0(strip + 2)
            ffn1_done_pe = s.cnt["pe"]
            cut("FFN1")

            for hs in range(4):
                for n in range(64):
                    gid = hs * 8 + n // 8
                    qid = hs * 16 + n // 4
                    if n % 8 == 0:
                        pn, v = w1_loaded[gid]
                        s.wait("pe", pn, v)
                    if n % 4 == 0:
                        pn, v = dr_loaded[qid]
                        s.wait("pe", pn, v)
                    for tb in range(ntc):
                        if n == 0:
                            pe_wait_bank(tb % 8)
                        mm(ps[:, tb % 8, 0:512],
                           drg[:, qid % 2, n % 4, tb * 128:(tb + 1) * 128],
                           w1r[:, gid % 2, n % 8, :],
                           start=(n == 0), stop=False,
                           inc="pe" if (tb == ntc - 1 and n % 4 == 3) else None)
                    if n % 4 == 3:
                        dr_marker[qid] = s.cnt["pe"]
                        if 2 <= qid + 2 < 4 * 16:
                            ffn2_load_dr(qid + 2)
                    if n % 8 == 7:
                        w1_marker[gid] = s.cnt["pe"]
                        if gid + 2 < NW1:
                            load_w1(gid + 2)
                if hs == 0:
                    s.wait("pe", *ld_b1)
                for tb in range(ntc):
                    mm(ps[:, tb % 8, 0:512], ones_sb[0:1, :],
                       b1bf[0:1, hs * 512:(hs + 1) * 512],
                       start=False, stop=True,
                       inc="pe" if tb == ntc - 1 else None)
                pm = s.cnt["pe"]
                s.wait("dv", "pe", pm)
                half_n = max(1, ntc // 2)
                for tb in range(ntc):
                    last_of_half = (tb == half_n - 1) or (tb == ntc - 1)
                    s.op("dv", lambda e, tb=tb, hs=hs: e.tensor_tensor(
                        x1[:, tb, hs * 512:(hs + 1) * 512],
                        x1[:, tb, hs * 512:(hs + 1) * 512],
                        ps[:, tb % 8, 0:512], ALU.add),
                        inc="dv" if last_of_half else None)
                    if last_of_half:
                        for bb in range(tb + 1):
                            bank_free[bb % 8] = ("dv", s.cnt["dv"])
                pe_done = s.cnt["pe"]
                prev_strip_pe = pe_done
                s.wait("sy", "dv", s.cnt["dv"])
                s.op("sy", lambda e, hs=hs: e.dma_start(
                    out=y3s[:, :, hs * 512:(hs + 1) * 512],
                    in_=x1[:, :, hs * 512:(hs + 1) * 512]), inc="yst", n=16)

            s.wait("sy", "yst", s.cnt["yst"])

        except _Cut:
            pass
        _finish[0]()

    return nc


def _rope_tables(core, sloc):
    ntc = sloc // 128
    inv = 10000.0 ** (-(np.arange(0, 32, 2, dtype=np.float64)) / 32.0)
    pos = core * sloc + np.arange(sloc, dtype=np.float64)
    ang = (pos / 40.0)[:, None] * inv[None, :]
    cos = np.cos(ang).astype(np.float32).reshape(ntc, 128, 16).transpose(1, 0, 2)
    sin = np.sin(ang).astype(np.float32).reshape(ntc, 128, 16).transpose(1, 0, 2)
    return np.ascontiguousarray(cos), np.ascontiguousarray(sin)


def kernel(L0_attn_x, L0_attn_W_dkv, L0_attn_W_dq, L0_attn_W_uk, L0_attn_W_uv,
           L0_attn_W_uq, L0_attn_W_kr, L0_attn_W_qr, L0_attn_W_out,
           L0_ffn_W0, L0_ffn_bias0, L0_ffn_W1, L0_ffn_bias1):
    n_cores, sloc = 8, 1024
    x = np.ascontiguousarray(np.asarray(L0_attn_x, dtype=np.float32).reshape(8192, D))
    f32 = lambda a: np.ascontiguousarray(np.asarray(a, dtype=np.float32))
    w = dict(w_dkv=f32(L0_attn_W_dkv), w_dq=f32(L0_attn_W_dq), w_uk=f32(L0_attn_W_uk),
             w_uv=f32(L0_attn_W_uv), w_uq=f32(L0_attn_W_uq), w_kr=f32(L0_attn_W_kr),
             w_qr=f32(L0_attn_W_qr), w_out=f32(L0_attn_W_out),
             w0t=np.ascontiguousarray(np.asarray(L0_ffn_W0, dtype=np.float32).T),
             w1t=np.ascontiguousarray(np.asarray(L0_ffn_W1, dtype=np.float32).T),
             b0=f32(L0_ffn_bias0), b1=f32(L0_ffn_bias1),
             idm=np.eye(128, dtype=np.float32))
    in_maps = []
    for c in range(n_cores):
        cos, sin = _rope_tables(c, sloc)
        m = dict(w)
        m["x"] = np.ascontiguousarray(x[c * sloc:(c + 1) * sloc])
        m["cos"] = cos
        m["sin"] = sin
        in_maps.append(m)
    nc = build_nc(n_cores=n_cores, sloc=sloc)
    res = run_bass_kernel_spmd(nc, in_maps, core_ids=list(range(n_cores)))
    y = np.concatenate([np.asarray(r["y"]) for r in res.results], axis=0)
    return y.reshape(1, 8192, D).astype(np.float32)


# version 12
# speedup vs baseline: 1.0437x; 1.0343x over previous
import math
import numpy as np
import concourse.bass as bass
import concourse.mybir as mybir
from concourse.bass_utils import run_bass_kernel_spmd

F32 = mybir.dt.float32
BF16 = mybir.dt.bfloat16
F16 = mybir.dt.float16
U8 = mybir.dt.uint8
AF = mybir.ActivationFunctionType
ALU = mybir.AluOpType

D = 2048
CD = 1024
NH = 16
FF = 8192
KC = D // 128
CDC = CD // 128
ISQ = 1.0 / math.sqrt(128.0)
LN_EPS = 1e-5


def _hb(ap2d, nh):
    return bass.AP(ap2d.tensor, ap2d.offset,
                   [list(ap2d.ap[0]), [0, nh], list(ap2d.ap[1])])


class Sched:

    ENGS = ("pe", "dv", "ac", "sy", "gp")

    def __init__(self, nc):
        self.nc = nc
        self.q = {k: [] for k in self.ENGS}
        self.sems = {}
        self.cnt = {}

    def sem(self, name):
        self.sems[name] = self.nc.alloc_semaphore(name=name)
        self.cnt[name] = 0

    def wait(self, eng, sname, val=None):
        v = self.cnt[sname] if val is None else val
        if v <= 0:
            return
        h = self.sems[sname]
        self.q[eng].append(lambda e, h=h, v=v: e.wait_ge(h, v))

    def drain(self, eng):
        self.q[eng].append(lambda e: e.drain())

    def op(self, eng, fn, inc=None, n=1):
        if inc is None:
            self.q[eng].append(fn)
            return None
        self.cnt[inc] += n
        h = self.sems[inc]
        self.q[eng].append(lambda e, fn=fn, h=h, n=n: fn(e).then_inc(h, n))
        return self.cnt[inc]


def build_nc(n_cores=8, sloc=1024, upto=None):
    ntc = sloc // 128
    th = min(512, sloc)
    nth = sloc // th
    nc = bass.Bass("TRN2", target_bir_lowering=False)

    x_in = nc.dram_tensor("x", [sloc, D], F32, kind="ExternalInput")
    w_dkv = nc.dram_tensor("w_dkv", [D, CD], F32, kind="ExternalInput")
    w_dq = nc.dram_tensor("w_dq", [D, CD], F32, kind="ExternalInput")
    w_uk = nc.dram_tensor("w_uk", [CD, CD], F32, kind="ExternalInput")
    w_uv = nc.dram_tensor("w_uv", [CD, D], F32, kind="ExternalInput")
    w_uq = nc.dram_tensor("w_uq", [CD, CD], F32, kind="ExternalInput")
    w_kr = nc.dram_tensor("w_kr", [D, CD], F32, kind="ExternalInput")
    w_qr = nc.dram_tensor("w_qr", [CD, CD], F32, kind="ExternalInput")
    w_out = nc.dram_tensor("w_out", [D, D], F32, kind="ExternalInput")
    w0t = nc.dram_tensor("w0t", [D, FF], F32, kind="ExternalInput")
    w1t = nc.dram_tensor("w1t", [FF, D], F32, kind="ExternalInput")
    b0 = nc.dram_tensor("b0", [FF], F32, kind="ExternalInput")
    b1 = nc.dram_tensor("b1", [D], F32, kind="ExternalInput")
    cos_in = nc.dram_tensor("cos", [128, ntc, 16], F32, kind="ExternalInput")
    sin_in = nc.dram_tensor("sin", [128, ntc, 16], F32, kind="ExternalInput")
    id_in = nc.dram_tensor("idm", [128, 128], F32, kind="ExternalInput")
    y_out = nc.dram_tensor("y", [sloc, D], F32, kind="ExternalOutput")
    cc_in = nc.dram_tensor("cc_in", [128, NH, 128], F32)
    cc_out = nc.dram_tensor("cc_out", [128, NH, 128], F32)
    dsp = nc.dram_tensor("dsp", [FF, sloc], BF16)

    KB = 1024
    o_cos, o_sin = 0, 512
    o_id = 1024
    o_stats = 1280
    o_b0 = o_stats + 512
    o_ra = o_b0 + 256
    o_rb = o_ra + 512
    o_mst = o_rb + 512
    o_msbf = o_mst + 8 * KB
    o_ones = o_msbf + 4 * KB
    o_b1bf = o_ones + 256
    R0_END = o_b1bf + 4 * KB + 768
    o_hfm = R0_END
    o_ckv = o_hfm + 2 * KC * sloc
    o_cq = o_ckv + 2 * CDC * sloc
    o_krope = o_cq
    o_qc = o_cq + 2 * CDC * sloc
    o_krstg = o_qc
    o_qtm = o_qc + 2 * CDC * sloc
    o_xld = o_qtm + 2 * ntc * 16 * 128
    o_xsqA = o_xld + 16 * KB
    o_vtm = o_xld
    o_kboth = o_xld
    o_wsl = o_xld + max(24 * KB, 2 * ntc * 2048, 2 * 16 * sloc)
    TOTAL = o_wsl + 32 * KB
    o_w1r_x = None
    o_wkra_x = None
    if o_wsl - o_xld >= 32 * KB:
        o_wdq = o_xld
    else:
        o_wdq = None
    assert TOTAL <= 207 * KB, TOTAL
    if sloc >= 1024:
        o_apre = o_hfm
        o_x1 = o_ckv
        assert o_x1 + 4 * ntc * 2048 <= o_xld + 24 * KB
        o_gtm = o_x1 + 4 * ntc * 2048
        o_gfm = o_kboth
        o_wout = o_wsl
        o_xsqB = o_wsl + 16 * KB
        o_b1bc = o_wsl
        o_ef16 = o_wsl + 8 * KB
        o_w0r = o_apre
        o_w1r = o_apre
        o_dst = o_gtm
        o_drg = o_gtm
    else:
        cur = [TOTAL]

        def _al(szb):
            o = cur[0]
            cur[0] += (szb + 31) // 32 * 32
            return o

        o_apre = _al(2 * 16 * sloc)
        o_x1 = _al(4 * ntc * 2048)
        o_gtm = _al(16 * KB)
        o_gfm = _al(2 * 16 * sloc)
        o_shared = _al(32 * KB)
        o_wdq = o_shared
        o_wout = o_shared
        o_w0r = o_shared
        o_w1r = o_shared
        o_xsqB = _al(8 * KB)
        o_dst = _al(2 * 8 * sloc)
        o_drg = _al(2 * 8 * sloc)
        o_wuk_x = _al(16 * KB)
        o_woutb_x = o_wuk_x
        o_w1r_x = o_wuk_x
        o_wkra_x = _al(16 * KB)
        TOTAL = cur[0]
        assert TOTAL <= 207 * KB, TOTAL

    with (
        nc.sbuf_tensor("arena", [128, TOTAL], U8) as arena,
        nc.psum_tensor("ps", [128, 8, 512], F32) as ps,
        nc.Block() as block,
    ):
        base = nc.lookup_mloc(arena).addr
        ps_addr = nc.lookup_mloc(ps).addr
        pstr = nc.virtual_psum_tensor("pstr", [128, 2, 1024], BF16)
        nc._tensor(pstr.name, [128, 2, 1024], BF16, type="PSUM")
        _m = nc.lookup_mloc(pstr)
        _m.allocated = True
        _m.addr = ps_addr + 6 * 2048

        def sb(name, off, shape, dtype):
            h = nc.virtual_sbuf_tensor(name, shape, dtype)
            nc._tensor(h.name, list(shape), dtype, type="SB")
            mloc = nc.lookup_mloc(h)
            mloc.allocated = True
            mloc.addr = base + off
            return h

        cos_sb = sb("cos_sb", o_cos, [128, ntc, 16], F32)
        sin_sb = sb("sin_sb", o_sin, [128, ntc, 16], F32)
        ident = sb("ident", o_id, [128, 128], BF16)
        stats = sb("stats", o_stats, [128, 128], F32)
        b0sb = sb("b0sb", o_b0, [128, 64], F32)
        ropeA = sb("ropeA", o_ra, [128, 8, 16], F32)
        ropeB = sb("ropeB", o_rb, [128, 8, 16], F32)
        htmp = sb("htmp", o_gtm, [128, 4, 2048], BF16)
        mstage = sb("mstage", o_mst, [128, NH, 128], F32)
        msb = sb("msb", o_mst, [128, NH, 128], F32)
        msbf = sb("msbf", o_msbf, [128, NH, 128], BF16)
        hfm = sb("hfm", o_hfm, [128, KC, sloc], BF16)
        ckv = sb("ckv", o_ckv, [128, CDC, sloc], BF16)
        cq = sb("cq", o_cq, [128, CDC, sloc], BF16)
        qc = sb("qc", o_qc, [128, CDC, sloc], BF16)
        qtm = sb("qtm", o_qtm, [128, ntc, NH, 128], BF16)
        krope = sb("krope", o_krope, [128, ntc, NH, 64], BF16)
        krstg = sb("krstg", o_krstg, [64, NH, sloc], BF16)
        xld = sb("xld", o_xld, [128, 4, 2048], F32)
        vtm = sb("vtm", o_vtm, [128, ntc, 2048], BF16)
        kboth = sb("kboth", o_kboth, [128, NH, sloc], BF16)
        wsl = sb("wsl", o_wsl, [128, 16, 1024], BF16)
        wdq_sb = sb("wdq_sb", o_wdq, [128, 16, 1024], BF16)
        wkr_a = sb("wkr_a", (o_qc if sloc >= 1024 else o_wkra_x), [128, 16, 512], BF16)
        wuv_a = sb("wuv_a", o_wsl, [128, CDC, 1024], BF16)
        wuv_b = sb("wuv_b", o_wsl + 16 * KB, [128, CDC, 1024], BF16)
        ones_sb = sb("ones_sb", o_ones, [1, 128], BF16)
        b1bf = sb("b1bf", o_b1bf, [1, D], BF16)
        wuq_sb = sb("wuq_sb", o_wsl, [128, CDC, 1024], BF16)
        wqr_sb = sb("wqr_sb", o_wsl + 16 * KB, [128, CDC, 1024], BF16)
        wuv_sb = sb("wuv_sb", o_wsl, [128, CDC, 2048], BF16)
        wuk_sb = sb("wuk_sb", (o_gtm if sloc >= 1024 else o_wuk_x), [128, CDC, 1024], BF16)
        apre = sb("apre", o_apre, [128, NH, sloc], BF16)
        x1 = sb("x1", o_x1, [128, ntc, 2048], F32)
        gtm = sb("gtm", o_gtm, [128, 4, 2048], BF16)
        gfm = sb("gfm", o_gfm, [128, KC, sloc], BF16)
        wouts = sb("wouts", o_wout, [128, 16, 512], BF16)
        wouts_b = sb("wouts_b", (o_gtm if sloc >= 1024 else o_woutb_x), [128, 16, 512], BF16)
        xsqB = sb("xsqB", o_xsqB, [128, 2048], F32)
        w0r = sb("w0r", o_w0r, [128, 2, 16, 512], BF16)
        w1r = sb("w1r", (o_wsl + 16 * KB if sloc >= 1024 else o_w1r_x), [128, 2, 8, 512], BF16)
        dst = sb("dst", o_dst, [128, 8, sloc], BF16)
        drg = sb("drg", (o_wsl if sloc >= 1024 else o_drg), [128, 2, 4, sloc], BF16)


        s = Sched(nc)
        for nm in ("pe", "dv", "ac", "gw0", "gw1", "gw2", "gw3", "idl", "tbl",
                   "xa", "xb", "cc", "cci", "msb", "krs", "x1i", "dsa", "dsb",
                   "w0a", "w0b", "w1a", "w1b", "dra", "drb", "yst", "b1l"):
            s.sem(nm)
        _gwi = [0]

        def wload(fn):
            pn = "gw%d" % (_gwi[0] % 4)
            _gwi[0] += 1
            return (pn, s.op("gp", fn, inc=pn, n=16))

        _finish = [None]

        def _mk_finish():
            def fin():
                @block.tensor
                def _(t):
                    for f in s.q["pe"]:
                        f(t)

                @block.vector
                def _(v):
                    for f in s.q["dv"]:
                        f(v)

                @block.scalar
                def _(a):
                    for f in s.q["ac"]:
                        f(a)

                @block.sync
                def _(sy):
                    for f in s.q["sy"]:
                        f(sy)

                @block.gpsimd
                def _(g):
                    for f in s.q["gp"]:
                        f(g)
            return fin
        _finish[0] = _mk_finish()

        class _Cut(Exception):
            pass

        def cut(tag):
            if upto == tag:
                raise _Cut()

        bank_free = [("dv", 0)] * 8
        slot_free = [("dv", 0)] * 8
        EPS_AP = stats[:, 120:121]
        try:

            def pe_wait_bank(b):
                sm, v = bank_free[b]
                s.wait("pe", sm, v)

            def mm(psum_ap, lhsT, rhs, start, stop, inc=None):
                return s.op("pe", lambda e, a=psum_ap, l=lhsT, r=rhs, st=start,
                            sp=stop: e.matmul(a, l, r, start=st, stop=sp), inc=inc)

            tbank_free = {6: ("dv", 0), 7: ("dv", 0)}

            def transpose_groups(n_tiles, src_fn, dst_fn, rows=128, upper=False):
                p0 = 64 if upper else 0
                tp = (0, 64) if upper else None
                ngr = (n_tiles + 7) // 8
                for g in range(ngr):
                    bk = 6 + (g % 2)
                    area = pstr[:, g % 2, :]
                    sm, v = tbank_free[bk]
                    s.wait("pe", sm, v)
                    lo = g * 8
                    hi = min(lo + 8, n_tiles)
                    for i in range(lo, hi):
                        sl = i - lo
                        s.op("pe", lambda e, i=i, sl=sl, area=area: e.transpose(
                            area[p0:p0 + rows, sl * 128:(sl + 1) * 128], src_fn(i),
                            ident[:, :], tile_position=tp),
                            inc="pe" if i == hi - 1 else None)
                    pm = s.cnt["pe"]
                    s.wait("dv", "pe", pm)
                    for i in range(lo, hi):
                        sl = i - lo
                        s.op("dv", lambda e, i=i, sl=sl, area=area: e.tensor_copy(
                            dst_fn(i), area[p0:p0 + rows, sl * 128:(sl + 1) * 128]),
                            inc="dv" if i == hi - 1 else None)
                    tbank_free[bk] = ("dv", s.cnt["dv"])

            s.op("dv", lambda e: e.memset(stats[:, 120:121], LN_EPS), inc="dv")
            s.op("gp", lambda e: e.dma_start(out=ident[:, :], in_=id_in[:, :]), inc="idl", n=16)
            s.op("sy", lambda e: e.dma_start(out=cos_sb[:, :, :], in_=cos_in[:, :, :]), inc="tbl", n=16)
            s.op("sy", lambda e: e.dma_start(out=sin_sb[:, :, :], in_=sin_in[:, :, :]), inc="tbl", n=16)
            tbl_all = s.cnt["tbl"]
            s.op("dv", lambda e: e.memset(ones_sb[0:1, :], 1.0))
            cut("A0")

            x3 = x_in.ap().rearrange("(tc p) d -> p tc d", p=128)
            htmp_free = [("pe", 0)] * 4
            SP, SQ, SM, SI, SV, SS2 = 0, 8, 16, 24, 32, 40
            p1_rd = {}
            for tc in range(ntc):
                par = tc % 4 if ntc >= 4 else tc % 2
                qeng, qsem = "sy", "xa"
                dep = 4 if ntc >= 4 else 2
                if tc >= dep:
                    a_c, d_c = p1_rd[tc - dep]
                    s.wait(qeng, "ac", a_c)
                    s.wait(qeng, "dv", d_c)
                xw = s.op(qeng, lambda e, t=tc, p=par: e.dma_start(
                    out=xld[:, p, :], in_=x3[:, t, :]), inc=qsem, n=16)
                s.wait("ac", qsem, xw)
                s.drain("ac")
                s.op("ac", lambda e, p=par, t=tc: e.activation(
                    htmp[:, p % 4 if ntc >= 4 else p % 2, :], xld[:, p, :], AF.Square,
                    accum_out=stats[:, SQ + t:SQ + t + 1]), inc="ac")
                sq_done = s.cnt["ac"]
                s.wait("dv", qsem, xw)
                s.op("dv", lambda e, p=par, t=tc: e.reduce_sum(
                    stats[:, SP + t:SP + t + 1], xld[:, p, :],
                    axis=mybir.AxisListType.X), inc="dv")
                p1_rd[tc] = (sq_done, s.cnt["dv"])
            s.wait("gp", "xa", s.cnt["xa"])
            ld_dkv = wload(lambda e: e.dma_start(
                out=wsl[:, :, :], in_=w_dkv.ap().rearrange("(k p) m -> p k m", p=128)))
            s.wait("dv", "ac", s.cnt["ac"])
            s.drain("dv")
            for tc in range(ntc):
                s.op("dv", lambda e, t=tc: e.tensor_single_scalar(
                    stats[:, SM + t:SM + t + 1], stats[:, SP + t:SP + t + 1],
                    1.0 / D, ALU.mult))
                s.op("dv", lambda e, t=tc: e.tensor_single_scalar(
                    stats[:, SS2 + t:SS2 + t + 1], stats[:, SQ + t:SQ + t + 1],
                    1.0 / D, ALU.mult))
            s.drain("dv")
            for tc in range(ntc):
                s.op("dv", lambda e, t=tc: e.tensor_tensor(
                    stats[:, SV + t:SV + t + 1], stats[:, SM + t:SM + t + 1],
                    stats[:, SM + t:SM + t + 1], ALU.mult))
            s.drain("dv")
            s.op("dv", lambda e: e.tensor_tensor(
                stats[:, SV:SV + ntc], stats[:, SS2:SS2 + ntc],
                stats[:, SV:SV + ntc], ALU.subtract), inc="dv")
            s.wait("ac", "dv", s.cnt["dv"])
            s.op("ac", lambda e: e.activation(
                stats[:, SS2:SS2 + ntc], stats[:, SV:SV + ntc], AF.Sqrt,
                bias=EPS_AP), inc="ac")
            s.wait("dv", "ac", s.cnt["ac"])
            s.op("dv", lambda e: e.reciprocal(
                stats[:, SI:SI + ntc], stats[:, SS2:SS2 + ntc]), inc="dv")
            s.drain("dv")
            inv_ready_dv = s.cnt["dv"]
            p1_ac_all = s.cnt["ac"]
            p2_norm = {}
            order = (list(range(4, ntc)) + list(range(4))) if ntc >= 8 else list(range(ntc))
            first = True
            for tc in order:
                par = tc % 4 if ntc >= 4 else tc % 2
                hp = par if ntc >= 4 else par % 2
                resident = ntc >= 8 and tc >= 4
                if resident:
                    xw = None
                else:
                    qeng, qsem = "sy", "xa"
                    if ntc >= 8:
                        s.wait(qeng, "dv", p2_norm.get(tc + 4, inv_ready_dv))
                        s.wait(qeng, "ac", p1_ac_all)
                    elif tc >= 2:
                        s.wait(qeng, "dv", p2_norm[tc - 2])
                    else:
                        s.wait(qeng, "dv", inv_ready_dv)
                        s.wait(qeng, "ac", p1_ac_all)
                    xw = s.op(qeng, lambda e, t=tc, p=par: e.dma_start(
                        out=xld[:, p, :], in_=x3[:, t, :]), inc=qsem, n=16)
                sm, v = htmp_free[hp]
                s.wait("dv", sm, v)
                if resident:
                    s.wait("dv", "dv", inv_ready_dv)
                    s.wait("dv", "ac", p1_ac_all)
                else:
                    s.wait("dv", qsem, xw)
                s.op("dv", lambda e, p=par, hp=hp, t=tc: e.tensor_scalar(
                    htmp[:, hp, :], xld[:, p, :], stats[:, SM + t:SM + t + 1],
                    stats[:, SI + t:SI + t + 1], ALU.subtract, ALU.mult), inc="dv")
                h_done = s.cnt["dv"]
                p2_norm[tc] = h_done
                s.wait("pe", "dv", h_done)
                if first:
                    s.wait("pe", "idl", 16)
                    first = False
                transpose_groups(
                    KC,
                    lambda i, hp=hp: htmp[:, hp, i * 128:(i + 1) * 128],
                    lambda i, t=tc: hfm[:, i, t * 128:(t + 1) * 128])
                htmp_free[hp] = ("pe", s.cnt["pe"])
            h_ready = s.cnt["dv"]
            s.op("sy", lambda e: e.dma_start(
                out=b0sb[:, :], in_=b0.ap().rearrange("(c p) -> p c", p=128),
                allow_slow_non_contiguous=True), inc="tbl", n=16)
            b0_cnt = s.cnt["tbl"]
            cut("A1")

            def proj_fm(dstT, w_sb, n_out_chunks, n_k, rhs_sb):
                for m in range(n_out_chunks):
                    for t in range(nth):
                        b = (m * nth + t) % 4
                        pe_wait_bank(b)
                        for k in range(n_k):
                            mm(ps[:, b, 0:th],
                               w_sb[:, k, m * 128:(m + 1) * 128],
                               rhs_sb[:, k, t * th:(t + 1) * th],
                               start=(k == 0), stop=(k == n_k - 1),
                               inc="pe" if k == n_k - 1 else None)
                        pm = s.cnt["pe"]
                        s.wait("dv", "pe", pm)
                        s.op("dv", lambda e, m=m, t=t, b=b: e.tensor_copy(
                            dstT[:, m, t * th:(t + 1) * th], ps[:, b, 0:th]), inc="dv")
                        bank_free[b] = ("dv", s.cnt["dv"])

            s.wait("gp", "dv", h_ready)
            s.wait("gp", "ac", s.cnt["ac"])
            ld_dq = wload(lambda e: e.dma_start(
                out=wdq_sb[:, :, :], in_=w_dq.ap().rearrange("(k p) m -> p k m", p=128)))
            s.wait("pe", *ld_dkv)
            s.wait("pe", "dv", h_ready)
            proj_fm(ckv, wsl, CDC, KC, hfm)
            ckv_done_pe = s.cnt["pe"]
            cut("A2")

            s.wait("gp", "pe", ckv_done_pe)
            ld_uq = wload(lambda e: e.dma_start(
                out=wuq_sb[:, :, :], in_=w_uq.ap().rearrange("(k p) m -> p k m", p=128)))
            ld_qr = wload(lambda e: e.dma_start(
                out=wqr_sb[:, :, :], in_=w_qr.ap().rearrange("(k p) m -> p k m", p=128)))
            s.wait("pe", *ld_dq)
            proj_fm(cq, wdq_sb, CDC, KC, hfm)
            cq_done_pe = s.cnt["pe"]

            s.wait("pe", *ld_uq)
            s.wait("pe", "dv")
            proj_fm(qc, wuq_sb, CDC, CDC, cq)
            cut("A4")

            s.wait("pe", "dv")
            for g in range((ntc * CDC) // 8):
                bk = 6 + (g % 2)
                area = pstr[:, g % 2, :]
                sm, v = tbank_free[bk]
                s.wait("pe", sm, v)
                for i8 in range(8):
                    i = g * 8 + i8
                    tc, j = i // CDC, i % CDC
                    s.op("pe", lambda e, tc=tc, j=j, sl=i8, area=area: e.transpose(
                        area[:, sl * 128:(sl + 1) * 128],
                        qc[:, j, tc * 128:(tc + 1) * 128], ident[:, :]),
                        inc="pe" if i8 == 7 else None)
                pm = s.cnt["pe"]
                s.wait("dv", "pe", pm)
                for i8 in range(8):
                    i = g * 8 + i8
                    tc, j = i // CDC, i % CDC
                    s.op("dv", lambda e, tc=tc, j=j, sl=i8, area=area: e.tensor_copy(
                        qtm[:, tc, 2 * j:2 * j + 2, 0:64],
                        area[:, sl * 128:(sl + 1) * 128].rearrange(
                            "p (h c) -> p h c", h=2)),
                        inc="dv" if i8 == 7 else None)
                tbank_free[bk] = ("dv", s.cnt["dv"])
            a5_done_pe = s.cnt["pe"]
            s.wait("gp", "pe", a5_done_pe)
            ld_uva = wload(lambda e: e.dma_start(
                out=wuv_a[:, :, :],
                in_=w_uv.ap().rearrange("(k p) m -> p k m", p=128)[:, :, 0:1024]))

            def rope_block(psum_ap, out_ap, tc):
                P = psum_ap.rearrange("p (h c) -> p h c", h=8)
                cl = _hb(cos_sb[:, tc, :], 8)
                sl_ = _hb(sin_sb[:, tc, :], 8)
                s.drain("dv")
                s.op("dv", lambda e, P=P, cl=cl: e.tensor_tensor(
                    ropeA[:, :, :], P[:, :, 0:16], cl, ALU.mult))
                s.op("dv", lambda e, P=P, sl_=sl_: e.tensor_tensor(
                    ropeB[:, :, :], P[:, :, 16:32], sl_, ALU.mult))
                s.drain("dv")
                s.op("dv", lambda e, out_ap=out_ap: e.tensor_tensor(
                    out_ap[:, :, 0:16], ropeA[:, :, :], ropeB[:, :, :], ALU.subtract))
                s.drain("dv")
                s.op("dv", lambda e, P=P, cl=cl: e.tensor_tensor(
                    ropeA[:, :, :], P[:, :, 16:32], cl, ALU.mult))
                s.op("dv", lambda e, P=P, sl_=sl_: e.tensor_tensor(
                    ropeB[:, :, :], P[:, :, 0:16], sl_, ALU.mult))
                s.drain("dv")
                s.op("dv", lambda e, out_ap=out_ap: e.tensor_tensor(
                    out_ap[:, :, 16:32], ropeA[:, :, :], ropeB[:, :, :], ALU.add))
                s.op("dv", lambda e, P=P, out_ap=out_ap: e.tensor_copy(
                    out_ap[:, :, 32:64], P[:, :, 32:64]), inc="dv")

            s.wait("dv", "tbl", b0_cnt)
            s.wait("pe", *ld_qr)
            s.wait("pe", "dv")
            for tc in range(ntc):
                for g in range(2):
                    b = (tc * 2 + g) % 4
                    pe_wait_bank(b)
                    for k in range(CDC):
                        mm(ps[:, b, 0:512], qc[:, k, tc * 128:(tc + 1) * 128],
                           wqr_sb[:, k, g * 512:(g + 1) * 512],
                           start=(k == 0), stop=(k == CDC - 1),
                           inc="pe" if k == CDC - 1 else None)
                    pm = s.cnt["pe"]
                    s.wait("dv", "pe", pm)
                    rope_block(ps[:, b, 0:512], qtm[:, tc, 8 * g:8 * g + 8, 64:128], tc)
                    bank_free[b] = ("dv", s.cnt["dv"])
            qr_done_pe = s.cnt["pe"]
            s.wait("gp", "pe", qr_done_pe)
            ld_kra = wload(lambda e: e.dma_start(
                out=wkr_a[:, :, :],
                in_=w_kr.ap().rearrange("(k p) m -> p k m", p=128)[:, :, 0:512]))
            cut("A6")

            s.wait("gp", "pe", qr_done_pe)
            ld_uvb = wload(lambda e: e.dma_start(
                out=wuv_b[:, :, :],
                in_=w_uv.ap().rearrange("(k p) m -> p k m", p=128)[:, :, 1024:2048]))
            v3 = vtm.ap().rearrange("p tc (vc c) -> p tc vc c", c=512)
            for vc in range(4):
                s.wait("pe", *(ld_uva if vc < 2 else ld_uvb))
                wv = wuv_a if vc < 2 else wuv_b
                vcl = vc % 2
                for tc in range(ntc):
                    b = (vc * ntc + tc) % 4
                    pe_wait_bank(b)
                    for k in range(CDC):
                        mm(ps[:, b, 0:512], ckv[:, k, tc * 128:(tc + 1) * 128],
                           wv[:, k, vcl * 512:(vcl + 1) * 512],
                           start=(k == 0), stop=(k == CDC - 1),
                           inc="pe" if k == CDC - 1 else None)
                    pm = s.cnt["pe"]
                    s.wait("dv", "pe", pm)
                    s.op("dv", lambda e, tc=tc, vc=vc, b=b: e.tensor_copy(
                        v3[:, tc, vc, :], ps[:, b, 0:512]), inc="dv")
                    bank_free[b] = ("dv", s.cnt["dv"])
            v_done_pe = s.cnt["pe"]
            cut("A7")

            s.wait("pe", "dv", s.cnt["dv"])
            for hp in range(NH // 2):
                pe_wait_bank(4)
                pe_wait_bank(5)
                for j in range(2):
                    h = 2 * hp + j
                    for tc in range(ntc):
                        mm(ps[:, 4 + j, 0:128], qtm[:, tc, h, :],
                           vtm[:, tc, h * 128:(h + 1) * 128],
                           start=(tc == 0), stop=(tc == ntc - 1),
                           inc="pe" if (j == 1 and tc == ntc - 1) else None)
                pm = s.cnt["pe"]
                s.wait("dv", "pe", pm)
                for j in range(2):
                    h = 2 * hp + j
                    s.op("dv", lambda e, h=h, j=j: e.tensor_single_scalar(
                        mstage[:, h, :], ps[:, 4 + j, 0:128], ISQ, ALU.mult),
                        inc="dv" if j == 1 else None)
                bank_free[4] = ("dv", s.cnt["dv"])
                bank_free[5] = ("dv", s.cnt["dv"])
            m_evac_done = s.cnt["dv"]
            cut("A8")

            m_done_pe = s.cnt["pe"]
            s.wait("sy", "dv", m_evac_done)
            s.op("sy", lambda e: e.dma_start(out=cc_in[:, :, :], in_=mstage[:, :, :]),
                 inc="cci", n=16)
            s.wait("gp", "pe", v_done_pe)
            ld_krb = wload(lambda e: e.dma_start(
                out=wsl[:, :, 0:512],
                in_=w_kr.ap().rearrange("(k p) m -> p k m", p=128)[:, :, 512:1024]))
            s.wait("gp", "cci", 16)
            s.op("gp", lambda e: e.collective_compute(
                "AllReduce", ALU.add, replica_groups=[list(range(n_cores))],
                ins=[cc_in.ap().opt()], outs=[cc_out.ap().opt()]), inc="cc")
            s.wait("sy", "cc", 1)
            s.op("sy", lambda e: e.dma_start(out=msb[:, :, :], in_=cc_out[:, :, :]),
                 inc="msb", n=16)
            s.wait("gp", "pe", m_done_pe)
            ld_uk = wload(lambda e: e.dma_start(
                out=wuk_sb[:, :, :], in_=w_uk.ap().rearrange("(k p) m -> p k m", p=128)))
            ld_b1 = ("b1l", s.op("gp", lambda e: e.dma_start(
                out=b1bf[0:1, :], in_=b1.ap().unsqueeze(0)), inc="b1l", n=16))

            for g in range(2):
                s.wait("pe", *(ld_kra if g == 0 else ld_krb))
                wk = wkr_a if g == 0 else wsl
                for tc in range(ntc):
                    b = (g * ntc + tc) % 4
                    pe_wait_bank(b)
                    for k in range(KC):
                        mm(ps[:, b, 0:512], hfm[:, k, tc * 128:(tc + 1) * 128],
                           wk[:, k, 0:512],
                           start=(k == 0), stop=(k == KC - 1),
                           inc="pe" if k == KC - 1 else None)
                    pm = s.cnt["pe"]
                    s.wait("dv", "pe", pm)
                    rope_block(ps[:, b, 0:512], krope[:, tc, 8 * g:8 * g + 8, 0:64], tc)
                    bank_free[b] = ("dv", s.cnt["dv"])
            kr_done_pe = s.cnt["pe"]
            krope_done_dv = s.cnt["dv"]
            cut("A11")

            s.wait("pe", "dv", krope_done_dv)
            transpose_groups(
                ntc * NH,
                lambda i: krope[:, i // NH, i % NH, :],
                lambda i: kboth[64:128, i % NH, (i // NH) * 128:(i // NH) * 128 + 128],
                rows=64, upper=True)
            krstg_done_dv = s.cnt["dv"]
            krT_done_pe = s.cnt["pe"]
            cut("A12")

            s.wait("pe", *ld_uk)
            for h in range(NH):
                for t in range(nth):
                    b = (h * nth + t) % 4
                    pe_wait_bank(b)
                    for k in range(CDC):
                        mm(ps[0:64, b, 0:th], wuk_sb[:, k, h * 64:(h + 1) * 64],
                           ckv[:, k, t * th:(t + 1) * th],
                           start=(k == 0), stop=(k == CDC - 1),
                           inc="pe" if k == CDC - 1 else None)
                    pm = s.cnt["pe"]
                    s.wait("ac", "pe", pm)
                    s.op("ac", lambda e, h=h, t=t, b=b: e.copy(
                        kboth[0:64, h, t * th:(t + 1) * th], ps[0:64, b, 0:th]), inc="ac")
                    bank_free[b] = ("ac", s.cnt["ac"])
            kc_done_pe = s.cnt["pe"]
            kc_done_ac = s.cnt["ac"]
            cut("A10")

            s.wait("sy", "dv", krstg_done_dv)
            s.wait("sy", "pe", kc_done_pe)
            s.op("sy", lambda e: e.dma_start(out=x1[:, :, :], in_=x3[:, :, :]),
                 inc="x1i", n=16)

            s.wait("dv", "msb", 16)
            s.op("dv", lambda e: e.tensor_copy(msbf[:, :, :], msb[:, :, :]), inc="dv")
            msbf_done = s.cnt["dv"]
            cut("AR")

            s.wait("pe", "dv", msbf_done)
            units = [(h, t) for h in range(NH) for t in range(nth)]
            for g0 in range(0, len(units), 8):
                grp = units[g0:g0 + 8]
                for b in range(min(8, len(grp))):
                    pe_wait_bank(b)
                for i, (h, t) in enumerate(grp):
                    mm(ps[:, i, 0:th], msbf[:, h, :], kboth[:, h, t * th:(t + 1) * th],
                       start=True, stop=True,
                       inc="pe" if i == len(grp) - 1 else None)
                pm = s.cnt["pe"]
                s.wait("dv", "pe", pm)
                s.wait("ac", "pe", pm)
                for i, (h, t) in enumerate(grp):
                    eng = "dv" if i % 2 == 0 else "ac"
                    fn = (lambda e, h=h, t=t, i=i: e.tensor_copy(
                        apre[:, h, t * th:(t + 1) * th], ps[:, i, 0:th])) \
                        if eng == "dv" else \
                        (lambda e, h=h, t=t, i=i: e.copy(
                            apre[:, h, t * th:(t + 1) * th], ps[:, i, 0:th]))
                    s.op(eng, fn, inc=eng if i >= len(grp) - 2 else None)
                for b in range(min(8, len(grp))):
                    bank_free[b] = ("dv", s.cnt["dv"]) if b % 2 == 0 else ("ac", s.cnt["ac"])

            s.wait("pe", "dv")
            s.wait("pe", "ac")
            wo4 = w_out.ap().rearrange("(m p) o -> p m o", p=128)
            wout_marker = {-2: kr_done_pe, -1: kc_done_pe}
            wout_loaded = {}

            def load_wout(og):
                s.wait("gp", "pe", wout_marker[og - 2])
                buf = wouts if og % 2 == 0 else wouts_b
                wout_loaded[og] = wload(lambda e, og=og, buf=buf: e.dma_start(
                    out=buf[:, :, :], in_=wo4[:, :, og * 512:(og + 1) * 512]))

            load_wout(0)
            load_wout(1)
            s.wait("dv", "x1i", 16)
            for og in range(4):
                s.wait("pe", *wout_loaded[og])
                wbuf = wouts if og % 2 == 0 else wouts_b
                for tc in range(ntc):
                    b = (og * ntc + tc) % 4
                    pe_wait_bank(b)
                    for m in range(16):
                        mm(ps[:, b, 0:512], apre[:, m, tc * 128:(tc + 1) * 128],
                           wbuf[:, m, :],
                           start=(m == 0), stop=(m == 15),
                           inc="pe" if m == 15 else None)
                    pm = s.cnt["pe"]
                    s.wait("dv", "pe", pm)
                    s.op("dv", lambda e, tc=tc, og=og, b=b: e.tensor_tensor(
                        x1[:, tc, og * 512:(og + 1) * 512],
                        x1[:, tc, og * 512:(og + 1) * 512],
                        ps[:, b, 0:512], ALU.add), inc="dv")
                    add_done = s.cnt["dv"]
                    bank_free[b] = ("dv", add_done)
                    s.drain("dv")
                    s.op("dv", lambda e, tc=tc, og=og: e.reduce_sum(
                        stats[:, 48 + tc * 4 + og:48 + tc * 4 + og + 1],
                        x1[:, tc, og * 512:(og + 1) * 512],
                        axis=mybir.AxisListType.X), inc="dv")
                    s.wait("ac", "dv", add_done)
                    s.drain("ac")
                    s.op("ac", lambda e, tc=tc, og=og: e.activation(
                        xsqB[:, 0:512], x1[:, tc, og * 512:(og + 1) * 512],
                        AF.Square,
                        accum_out=stats[:, 80 + tc * 4 + og:80 + tc * 4 + og + 1]),
                        inc="ac")
                wout_marker[og] = s.cnt["pe"]
                if og + 2 < 4:
                    load_wout(og + 2)
            b12_done_pe_v = s.cnt["pe"]
            attn_done_dv = s.cnt["dv"]
            cut("B12")

            ln2_parts_dv = s.cnt["dv"]
            ln2_parts_ac = s.cnt["ac"]
            s.drain("dv")
            for tc in range(ntc):
                s.op("dv", lambda e, t=tc: e.reduce_sum(
                    stats[:, SP + t:SP + t + 1], stats[:, 48 + t * 4:48 + t * 4 + 4],
                    axis=mybir.AxisListType.X))
            s.wait("dv", "ac", ln2_parts_ac)
            for tc in range(ntc):
                s.op("dv", lambda e, t=tc: e.reduce_sum(
                    stats[:, SQ + t:SQ + t + 1], stats[:, 80 + t * 4:80 + t * 4 + 4],
                    axis=mybir.AxisListType.X))
            s.drain("dv")
            for tc in range(ntc):
                s.op("dv", lambda e, t=tc: e.tensor_single_scalar(
                    stats[:, SM + t:SM + t + 1], stats[:, SP + t:SP + t + 1],
                    1.0 / D, ALU.mult))
                s.op("dv", lambda e, t=tc: e.tensor_single_scalar(
                    stats[:, SS2 + t:SS2 + t + 1], stats[:, SQ + t:SQ + t + 1],
                    1.0 / D, ALU.mult))
            s.drain("dv")
            for tc in range(ntc):
                s.op("dv", lambda e, t=tc: e.tensor_tensor(
                    stats[:, SV + t:SV + t + 1], stats[:, SM + t:SM + t + 1],
                    stats[:, SM + t:SM + t + 1], ALU.mult))
            s.drain("dv")
            s.op("dv", lambda e: e.tensor_tensor(
                stats[:, SV:SV + ntc], stats[:, SS2:SS2 + ntc],
                stats[:, SV:SV + ntc], ALU.subtract), inc="dv")
            s.wait("ac", "dv", s.cnt["dv"])
            s.op("ac", lambda e: e.activation(
                stats[:, SS2:SS2 + ntc], stats[:, SV:SV + ntc], AF.Sqrt,
                bias=EPS_AP), inc="ac")
            s.wait("dv", "ac", s.cnt["ac"])
            s.op("dv", lambda e: e.reciprocal(
                stats[:, SI:SI + ntc], stats[:, SS2:SS2 + ntc]), inc="dv")
            s.drain("dv")
            gtm_free = [("pe", 0)] * 4
            for tc in range(ntc):
                par = tc % 4 if ntc >= 4 else tc % 2
                sm, v = gtm_free[par]
                s.wait("dv", sm, v)
                s.op("dv", lambda e, t=tc, p=par: e.tensor_scalar(
                    gtm[:, p, :], x1[:, t, :], stats[:, SM + t:SM + t + 1],
                    stats[:, SI + t:SI + t + 1], ALU.subtract, ALU.mult), inc="dv")
                g_done = s.cnt["dv"]
                s.wait("pe", "dv", g_done)
                transpose_groups(
                    KC,
                    lambda i, p=par: gtm[:, p, i * 128:(i + 1) * 128],
                    lambda i, t=tc: gfm[:, i, t * 128:(t + 1) * 128])
                gtm_free[par] = ("pe", s.cnt["pe"])
            g_ready_dv = s.cnt["dv"]
            b13_pe = s.cnt["pe"]
            b13_ac = s.cnt["ac"]
            cut("B13")

            w0v = w0t.ap().rearrange("(k p) n -> p k n", p=128)
            NS = FF // 512
            b12_done_pe = b12_done_pe_v
            w0_marker = {-2: b12_done_pe, -1: b12_done_pe}
            dst_free = [("ac", 0), ("ac", 0)]
            dsp4 = dsp.ap().rearrange("(b n4 p) t -> p b n4 t", p=128, n4=4)

            w0_loaded = {}
            store_done = {}

            def load_w0(strip):
                pn = "w0a" if strip % 2 == 0 else "w0b"
                s.wait("gp", "pe", w0_marker[strip - 2])
                w0_loaded[strip] = (pn, s.op(
                    "gp", lambda e, st=strip, p=strip % 2: e.dma_start(
                        out=w0r[:, p, :, :], in_=w0v[:, :, st * 512:(st + 1) * 512]),
                    inc=pn, n=16))

            load_w0(0)
            load_w0(1)
            y3s = y_out.ap().rearrange("(tc p) d -> p tc d", p=128)
            w1v = w1t.ap().rearrange("(g8 n2 p) h -> p g8 n2 h", p=128, n2=8)
            NW1 = 4 * (FF // 1024)
            w1_marker = {-2: b13_pe, -1: b13_pe}
            dr_marker = {-2: b13_pe, -1: b13_pe}
            w1_loaded = {}
            dr_loaded = {}

            def load_w1(gid):
                pn = "w1a" if gid % 2 == 0 else "w1b"
                s.wait("gp", "pe", w1_marker[gid - 2])
                if gid < 2:
                    s.wait("gp", "ac", b13_ac)
                w1_loaded[gid] = (pn, s.op(
                    "gp", lambda e, gid=gid: e.dma_start(
                        out=w1r[:, gid % 2, :, :],
                        in_=w1v[:, gid % 8, :, (gid // 8) * 512:(gid // 8) * 512 + 512]),
                    inc=pn, n=16))

            def ffn2_load_dr(qid):
                pn = "dra" if qid % 2 == 0 else "drb"
                s.wait("sy", "pe", dr_marker[qid - 2])
                q = qid % 16
                s.wait("sy", *store_done[q])
                dr_loaded[qid] = (pn, s.op(
                    "sy", lambda e, qid=qid, q=q: e.dma_start(
                        out=drg[:, qid % 2, :, :], in_=dsp4[:, q, :, :]), inc=pn, n=16))

            load_w1(0)
            load_w1(1)
            s.wait("pe", "dv", g_ready_dv)
            s.wait("ac", "tbl", b0_cnt)
            for strip in range(NS):
                pn, v = w0_loaded[strip]
                s.wait("pe", pn, v)
                for ncl in range(4):
                    gnc = strip * 4 + ncl
                    slot = gnc % 8
                    half = slot // 4
                    banks = [(gnc * nth + t) % 4 for t in range(nth)]
                    for t in range(nth):
                        pe_wait_bank(banks[t])
                    for k in range(KC):
                        for t in range(nth):
                            mm(ps[:, banks[t], 0:th],
                               w0r[:, strip % 2, k, ncl * 128:(ncl + 1) * 128],
                               gfm[:, k, t * th:(t + 1) * th],
                               start=(k == 0), stop=(k == KC - 1),
                               inc="pe" if (k == KC - 1 and t == nth - 1) else None)
                    pm = s.cnt["pe"]
                    sm, v = dst_free[half]
                    s.wait("ac", sm, v)
                    s.wait("ac", "pe", pm)
                    for t in range(nth):
                        s.op("ac", lambda e, gnc=gnc, slot=slot, t=t, b=banks[t]: e.activation(
                            dst[:, slot, t * th:(t + 1) * th], ps[:, b, 0:th], AF.Gelu,
                            bias=b0sb[:, gnc:gnc + 1]),
                            inc="ac" if t == nth - 1 else None)
                    for t in range(nth):
                        bank_free[banks[t]] = ("ac", s.cnt["ac"])
                    if slot % 4 == 3:
                        ac_now = s.cnt["ac"]
                        s.wait("sy", "ac", ac_now)
                        bidx = gnc // 4
                        sname = "dsa" if half == 0 else "dsb"
                        s.op("sy", lambda e, bidx=bidx, half=half: e.dma_start(
                            out=dsp4[:, bidx, :, :],
                            in_=dst[:, half * 4:half * 4 + 4, :]), inc=sname, n=16)
                        dst_free[half] = (sname, s.cnt[sname])
                        store_done[bidx] = (sname, s.cnt[sname])
                        if bidx < 2:
                            ffn2_load_dr(bidx)
                w0_marker[strip] = s.cnt["pe"]
                if strip + 2 < NS:
                    load_w0(strip + 2)
            ffn1_done_pe = s.cnt["pe"]
            cut("FFN1")

            for hs in range(4):
                for n in range(64):
                    gid = hs * 8 + n // 8
                    qid = hs * 16 + n // 4
                    if n % 8 == 0:
                        pn, v = w1_loaded[gid]
                        s.wait("pe", pn, v)
                    if n % 4 == 0:
                        pn, v = dr_loaded[qid]
                        s.wait("pe", pn, v)
                    for tb in range(ntc):
                        if n == 0:
                            pe_wait_bank(tb % 8)
                        mm(ps[:, tb % 8, 0:512],
                           drg[:, qid % 2, n % 4, tb * 128:(tb + 1) * 128],
                           w1r[:, gid % 2, n % 8, :],
                           start=(n == 0), stop=False,
                           inc="pe" if (tb == ntc - 1 and n % 4 == 3) else None)
                    if n % 4 == 3:
                        dr_marker[qid] = s.cnt["pe"]
                        if 2 <= qid + 2 < 4 * 16:
                            ffn2_load_dr(qid + 2)
                    if n % 8 == 7:
                        w1_marker[gid] = s.cnt["pe"]
                        if gid + 2 < NW1:
                            load_w1(gid + 2)
                if hs == 0:
                    s.wait("pe", *ld_b1)
                for tb in range(ntc):
                    mm(ps[:, tb % 8, 0:512], ones_sb[0:1, :],
                       b1bf[0:1, hs * 512:(hs + 1) * 512],
                       start=False, stop=True,
                       inc="pe" if tb == ntc - 1 else None)
                pm = s.cnt["pe"]
                s.wait("dv", "pe", pm)
                half_n = max(1, ntc // 2)
                for tb in range(ntc):
                    last_of_half = (tb == half_n - 1) or (tb == ntc - 1)
                    s.op("dv", lambda e, tb=tb, hs=hs: e.tensor_tensor(
                        x1[:, tb, hs * 512:(hs + 1) * 512],
                        x1[:, tb, hs * 512:(hs + 1) * 512],
                        ps[:, tb % 8, 0:512], ALU.add),
                        inc="dv" if last_of_half else None)
                    if last_of_half:
                        for bb in range(tb + 1):
                            bank_free[bb % 8] = ("dv", s.cnt["dv"])
                pe_done = s.cnt["pe"]
                prev_strip_pe = pe_done
                s.wait("sy", "dv", s.cnt["dv"])
                s.op("sy", lambda e, hs=hs: e.dma_start(
                    out=y3s[:, :, hs * 512:(hs + 1) * 512],
                    in_=x1[:, :, hs * 512:(hs + 1) * 512]), inc="yst", n=16)

            s.wait("sy", "yst", s.cnt["yst"])

        except _Cut:
            pass
        _finish[0]()

    return nc


def _rope_tables(core, sloc):
    ntc = sloc // 128
    inv = 10000.0 ** (-(np.arange(0, 32, 2, dtype=np.float64)) / 32.0)
    pos = core * sloc + np.arange(sloc, dtype=np.float64)
    ang = (pos / 40.0)[:, None] * inv[None, :]
    cos = np.cos(ang).astype(np.float32).reshape(ntc, 128, 16).transpose(1, 0, 2)
    sin = np.sin(ang).astype(np.float32).reshape(ntc, 128, 16).transpose(1, 0, 2)
    return np.ascontiguousarray(cos), np.ascontiguousarray(sin)


def kernel(L0_attn_x, L0_attn_W_dkv, L0_attn_W_dq, L0_attn_W_uk, L0_attn_W_uv,
           L0_attn_W_uq, L0_attn_W_kr, L0_attn_W_qr, L0_attn_W_out,
           L0_ffn_W0, L0_ffn_bias0, L0_ffn_W1, L0_ffn_bias1):
    n_cores, sloc = 8, 1024
    x = np.ascontiguousarray(np.asarray(L0_attn_x, dtype=np.float32).reshape(8192, D))
    f32 = lambda a: np.ascontiguousarray(np.asarray(a, dtype=np.float32))
    w = dict(w_dkv=f32(L0_attn_W_dkv), w_dq=f32(L0_attn_W_dq), w_uk=f32(L0_attn_W_uk),
             w_uv=f32(L0_attn_W_uv), w_uq=f32(L0_attn_W_uq), w_kr=f32(L0_attn_W_kr),
             w_qr=f32(L0_attn_W_qr), w_out=f32(L0_attn_W_out),
             w0t=np.ascontiguousarray(np.asarray(L0_ffn_W0, dtype=np.float32).T),
             w1t=np.ascontiguousarray(np.asarray(L0_ffn_W1, dtype=np.float32).T),
             b0=f32(L0_ffn_bias0), b1=f32(L0_ffn_bias1),
             idm=np.eye(128, dtype=np.float32))
    in_maps = []
    for c in range(n_cores):
        cos, sin = _rope_tables(c, sloc)
        m = dict(w)
        m["x"] = np.ascontiguousarray(x[c * sloc:(c + 1) * sloc])
        m["cos"] = cos
        m["sin"] = sin
        in_maps.append(m)
    nc = build_nc(n_cores=n_cores, sloc=sloc)
    res = run_bass_kernel_spmd(nc, in_maps, core_ids=list(range(n_cores)))
    y = np.concatenate([np.asarray(r["y"]) for r in res.results], axis=0)
    return y.reshape(1, 8192, D).astype(np.float32)
